# version 03
# baseline (speedup 1.0000x reference)
import contextlib
import math
import os
import numpy as np
import ml_dtypes

import concourse.bass as bass
import concourse.mybir as mybir
from concourse.bass_isa import ReduceOp
from concourse.bass_utils import run_bass_kernel_spmd

F32 = mybir.dt.float32
BF16 = mybir.dt.bfloat16
AF = mybir.ActivationFunctionType
ALU = mybir.AluOpType
AX = mybir.AxisListType.X

B, L, D = 2, 8192, 2048
NH, DH, S = 32, 64, 16
W = 64
TOK = 2048
HALO = 64
TT = TOK + 2 * HALO
NB = TT // W
NT = TT // 128
NTO = TOK // 128
INNER = 4 * D
NHG = 8
HPG = NH // NHG
MAGIC = float(3 * 2.0 ** 22)
RD = 1.0 / D

DBG = []
KPH = int(os.environ.get("KPH", "99"))


def _bf(x):
    return np.ascontiguousarray(x).astype(ml_dtypes.bfloat16)


def make_core_inputs(i, ins):
    b, s0 = i // 4, (i % 4) * TOK
    x = np.asarray(ins["L0_attn_u"])
    xs = np.zeros((TT, D), np.float32)
    lo, hi = s0 - HALO, s0 + TOK + HALO
    clo, chi = max(lo, 0), min(hi, L)
    xs[clo - lo: clo - lo + (chi - clo)] = x[b, clo:chi]

    Wqkv = np.asarray(ins["L0_attn_Wqkv"])
    bqkv = np.asarray(ins["L0_attn_bqkv"])
    perm = []
    for hg in range(NHG):
        for half in range(2):
            for hh in range(HPG):
                h = hg * HPG + hh
                j0 = half * 32
                perm.extend(range(h * DH + j0, h * DH + j0 + 32))
    perm = np.array(perm)
    WqT = np.ascontiguousarray(Wqkv[perm, :].T)
    WkT = np.ascontiguousarray(Wqkv[D + perm, :].T)
    WvT = np.ascontiguousarray(Wqkv[2 * D:, :].T)
    bq = np.ascontiguousarray(bqkv[perm].reshape(16, 128))
    bk = np.ascontiguousarray(bqkv[D + perm].reshape(16, 128))
    bv = np.ascontiguousarray(bqkv[2 * D:].reshape(1, D))

    WoT = np.ascontiguousarray(np.asarray(ins["L0_attn_Wo"]).T)
    bo = np.ascontiguousarray(np.asarray(ins["L0_attn_bo"]).reshape(1, D))

    sinks = np.asarray(ins["L0_attn_sinks"])
    sinksA = np.ascontiguousarray(np.transpose(sinks[:, :, :32], (0, 2, 1)))
    sinksB = np.ascontiguousarray(np.transpose(sinks[:, :, 32:], (0, 2, 1)))
    sinksV = np.ascontiguousarray(sinks)

    w1T = np.ascontiguousarray(np.asarray(ins["L0_ffn_w1"]).T)
    w2T = np.ascontiguousarray(np.asarray(ins["L0_ffn_w2"]).T)
    wmod = np.ascontiguousarray(np.asarray(ins["L0_wmod"]))
    c_row = np.ascontiguousarray(np.asarray(ins["L0_c"])[b].reshape(1, D))

    pos = np.arange(TT, dtype=np.float64) + lo
    inv = 1.0 / (10000.0 ** (np.arange(32, dtype=np.float64) / 32.0))
    ang = pos[None, :] * inv[:, None]
    cos = np.cos(ang); sin = np.sin(ang)
    c128 = np.tile(cos, (4, 1)); s128 = np.tile(sin, (4, 1))
    tabs = dict(ck=_bf(c128), sk=_bf(s128))

    r = np.arange(W)[:, None]
    c = np.arange(3 * W)[None, :]
    m_int = ((c - r >= 0) & (c - r <= 128)).astype(np.float32)
    c2 = np.arange(2 * W)[None, :]
    m_e0 = ((c2 - r >= -64) & (c2 - r <= 64)).astype(np.float32)
    m_all = np.zeros((64, NB * 192), np.float32)
    for m in range(NB):
        if m == 0:
            m_all[:, 0:128] = m_e0
        elif m == NB - 1:
            m_all[:, 192 * m: 192 * m + 128] = m_int[:, :128]
        else:
            m_all[:, 192 * m: 192 * m + 192] = m_int

    kill0 = np.full((64, 1), 1.0 if (s0 - HALO) >= 0 else 0.0, np.float32)
    kill33 = np.full((64, 1), 1.0 if (s0 + TOK + HALO) <= L else 0.0, np.float32)
    ident = _bf(np.eye(128, dtype=np.float32))

    return dict(
        x_sl=xs, WqT=WqT, WkT=WkT, WvT=WvT, bq=bq, bk=bk, bv=bv,
        WoT=WoT, bo=bo, sinksA=sinksA, sinksB=sinksB, sinksV=sinksV,
        w1T=w1T, w2T=w2T, wmod=wmod, c_row=c_row,
        m_all=_bf(m_all),
        kill0=kill0, kill33=kill33, ident=ident, **tabs,
    )


class Chain:
    def __init__(self, sem, name):
        self.sem = sem
        self.name = name
        self.n = 0

    def dep(self):
        return (self.sem, self.n, self.name)


class Prog:
    ENGINES = ("tensor", "vector", "scalar", "gpsimd", "sync")
    AUTO = {"vector", "scalar"}

    def __init__(self, chains):
        self.ops = {e: [] for e in self.ENGINES}
        self.last = {e: {} for e in self.ENGINES}
        self.ch = chains

    def _norm(self, d):
        if isinstance(d, Chain):
            return d.dep()
        if len(d) == 2:
            sem, val = d
            return (sem, val, str(id(sem)))
        return d

    def emit(self, eng, fn, deps=(), inc=None, inc_amt=1):
        lw = self.last[eng]
        deps = [self._norm(d) for d in deps]
        if eng in self.AUTO:
            ch = self.ch[eng]
            deps.append(ch.dep())
            inc, inc_amt = ch, 1
        waits = []
        for sem, val, key in deps:
            if val > lw.get(key, -1):
                lw[key] = val
                waits.append((sem, val))
        if inc is not None:
            inc.n += inc_amt
        inc_sem, amt = (inc.sem if inc else None), inc_amt

        def run(e):
            for sem, val in waits:
                e.wait_ge(sem, val)
            inst = fn(e)
            if inc_sem is not None:
                inst.then_inc(inc_sem, amt)
        self.ops[eng].append(run)
        return inc.n if inc is not None else None

    def dma(self, eng, dst, src, ch, deps=(), **kw):
        self.emit(eng, lambda e: e.dma_start(dst, src, **kw), deps=deps,
                  inc=ch, inc_amt=16)
        return ch.dep()


def build():
    nc = bass.Bass("TRN2", target_bir_lowering=False)
    A = lambda name, shape, dt, kind: nc.dram_tensor(name, shape, dt, kind=kind).ap()
    x_sl = A("x_sl", [TT, D], F32, "ExternalInput")
    WqT_i = A("WqT", [D, D], F32, "ExternalInput")
    WkT_i = A("WkT", [D, D], F32, "ExternalInput")
    WvT_i = A("WvT", [D, D], F32, "ExternalInput")
    bq_i = A("bq", [16, 128], F32, "ExternalInput")
    bk_i = A("bk", [16, 128], F32, "ExternalInput")
    bv_i = A("bv", [1, D], F32, "ExternalInput")
    WoT_i = A("WoT", [D, D], F32, "ExternalInput")
    bo_i = A("bo", [1, D], F32, "ExternalInput")
    sinksA_i = A("sinksA", [NH, 32, S], F32, "ExternalInput")
    sinksB_i = A("sinksB", [NH, 32, S], F32, "ExternalInput")
    sinksV_i = A("sinksV", [NH, S, DH], F32, "ExternalInput")
    w1T_i = A("w1T", [D, INNER], F32, "ExternalInput")
    w2T_i = A("w2T", [INNER, D], F32, "ExternalInput")
    wmod_i = A("wmod", [D, 6 * D], F32, "ExternalInput")
    c_i = A("c_row", [1, D], F32, "ExternalInput")
    ck_i = A("ck", [128, TT], BF16, "ExternalInput")
    sk_i = A("sk", [128, TT], BF16, "ExternalInput")
    m_all_i = A("m_all", [64, NB * 192], BF16, "ExternalInput")
    kill0_i = A("kill0", [64, 1], F32, "ExternalInput")
    kill33_i = A("kill33", [64, 1], F32, "ExternalInput")
    ident_i = A("ident", [128, 128], BF16, "ExternalInput")

    h_s = A("h_s", [TT, D], BF16, "Internal")
    y_s = A("y_s", [TOK, D], BF16, "Internal")
    x2_s = A("x2_s", [TOK, D], F32, "Internal")
    Xr1_s = A("Xr1_s", [TOK, D], BF16, "Internal")
    w1r_s = A("w1r_s", [D, INNER], BF16, "Internal")
    w2r_s = A("w2r_s", [INNER, D], BF16, "Internal")
    u2_s = A("u2_s", [TOK, INNER], BF16, "Internal")
    Xr2_s = A("Xr2_s", [TOK, INNER], BF16, "Internal")
    rs_s = A("rs_s", [2, 128], F32, "Internal")
    out = A("out", [TOK, D], F32, "ExternalOutput")
    dbg = {}
    if "mod" in DBG:
        dbg["d_mod"] = A("d_mod", [1, 6 * D], F32, "ExternalOutput")
    if "h" in DBG:
        dbg["d_h"] = A("d_h", [TT, D], BF16, "ExternalOutput")
    if "y" in DBG:
        dbg["d_y"] = A("d_y", [TOK, D], BF16, "ExternalOutput")
    if "ffn" in DBG:
        dbg["d_x2"] = A("d_x2", [TOK, D], F32, "ExternalOutput")
        dbg["d_xr1"] = A("d_xr1", [128, D], BF16, "ExternalOutput")
        dbg["d_u2"] = A("d_u2", [128, INNER], BF16, "ExternalOutput")
        dbg["d_xr2"] = A("d_xr2", [128, INNER], BF16, "ExternalOutput")
        dbg["d_w1r"] = A("d_w1r", [128, INNER], BF16, "ExternalOutput")
        dbg["d_w2r"] = A("d_w2r", [128, D], BF16, "ExternalOutput")
        dbg["d_drs"] = A("d_drs", [128, 16], F32, "ExternalOutput")
        dbg["d_wsc"] = A("d_wsc", [128, 4], F32, "ExternalOutput")
    if "att0" in DBG:
        dbg["d_ph"] = A("d_ph", [64, NB * 192], BF16, "ExternalOutput")
        dbg["d_ps"] = A("d_ps", [16, TT], BF16, "ExternalOutput")
        dbg["d_qa"] = A("d_qa", [32, TT], BF16, "ExternalOutput")
        dbg["d_qb"] = A("d_qb", [32, TT], BF16, "ExternalOutput")
        dbg["d_ka"] = A("d_ka", [32, TT], BF16, "ExternalOutput")
        dbg["d_kb"] = A("d_kb", [32, TT], BF16, "ExternalOutput")
        dbg["d_stk"] = A("d_stk", [128, 17 * 256], BF16, "ExternalOutput")
        dbg["d_vb"] = A("d_vb", [128, NT * HPG * 65], BF16, "ExternalOutput")
        dbg["d_yt"] = A("d_yt", [128, NTO * HPG * 64], BF16, "ExternalOutput")

    with contextlib.ExitStack() as es:
        E = es.enter_context
        Tn = lambda name, shape, dt: E(nc.sbuf_tensor(name, shape, dt))

        CH = {e: Chain(E(nc.semaphore(f"ch_{e}")), f"ch_{e}") for e in Prog.ENGINES}
        P = Prog(CH)
        cT, cV, cS = CH["tensor"], CH["vector"], CH["scalar"]
        dma = {nm: Chain(E(nc.semaphore(f"d_{nm}")), f"d_{nm}")
               for nm in ("c", "wm0", "wm1", "x0", "x1", "h0", "h1", "dbg", "out", "misc",
                          "cst", "ws", "snk", "hf0", "hf1", "yt",
                          "yld", "wol", "xa0", "xa1", "x2a0", "x2a1", "xr10", "xr11",
                          "wfa0", "wfa1", "wr0", "wr1", "w1a0", "w1a1", "x1T",
                          "u2s0", "u2s1", "u2a0", "u2a1", "x2r0", "x2r1", "x2T",
                          "w2a0", "w2a1", "x2b0", "x2b1", "ob0", "ob1", "rsa", "rsg")}

        ones1 = Tn("ones1", [1, 128], BF16)
        onecol = Tn("onecol", [128, 1], F32)
        epscol = Tn("epscol", [128, 1], F32)
        saF = Tn("saF", [128, D], BF16)
        ca1F = Tn("ca1F", [128, D], BF16)
        sbF = Tn("sbF", [128, D], BF16)
        cb1F = Tn("cb1F", [128, D], BF16)
        gaF = Tn("gaF", [128, D], BF16)
        gbF = Tn("gbF", [128, D], BF16)
        stm = Tn("stm", [128, 8], F32)
        id_sb = Tn("id_sb", [128, 128], BF16)
        st2 = Tn("st2", [128, 12], F32)
        drs1_all = Tn("drs1_all", [128, 16], F32)
        s1_all = Tn("s1_all", [128, 16], F32)
        wsc_c = Tn("wsc_c", [128, 4], F32)
        rs_c = Tn("rs_c", [128, 12], F32)

        psb = [E(nc.psum_tensor(f"psb{k}", [128, 512], F32)) for k in range(7)]
        ps_tr = E(nc.psum_tensor("ps_tr", [128, 1024], BF16))

        es01 = contextlib.ExitStack()
        T1 = lambda name, shape, dt: es01.enter_context(nc.sbuf_tensor(name, shape, dt))
        modrow = T1("modrow", [1, 6 * D], F32)
        modrow_bf = T1("modrow_bf", [1, 6 * D], BF16)
        gtmp = T1("gtmp", [1, 2 * D], F32)
        gtmp2 = T1("gtmp2", [1, 2 * D], F32)
        c_sil = T1("c_sil", [128, 16], BF16)
        c_silf = T1("c_silf", [128, 16], F32)
        wm_st = [T1(f"wm_st{k}", [128, 4 * 512], BF16) for k in range(2)]
        x_st = [T1(f"x_st{k}", [128, D], F32) for k in range(2)]
        sqscr = T1("sqscr", [128, D], F32)
        xh_t = [T1(f"xh_t{k}", [128, D], BF16) for k in range(2)]
        h_t = [T1(f"h_t{k}", [128, D], BF16) for k in range(2)]

        P.emit("vector", lambda v: v.memset(ones1[:1, :], 1.0))
        P.emit("vector", lambda v: v.memset(onecol[:, :], 1.0))
        P.emit("vector", lambda v: v.memset(epscol[:, :], 1e-5))
        k_consts = cV.n

        d_c = P.dma("gpsimd", c_silf[:, :], c_i.rearrange("o (f p) -> p (o f)", p=128),
                    dma["c"], allow_slow_non_contiguous=True)
        k_sil = P.emit("scalar", lambda s: s.activation(c_sil[:, :], c_silf[:, :], AF.Silu),
                       deps=[d_c])

        NMOD = 24
        evdone = []
        consumed_at = {}
        for og in range(NMOD):
            for i in range(16):
                n = og * 16 + i
                bch, k = n // 4, n % 4
                if k == 0:
                    ldeps = []
                    if bch >= 2:
                        ldeps.append((cT.sem, consumed_at[bch - 2], cT.name))
                    for kk in range(4):
                        nn = bch * 4 + kk
                        og2, i2 = nn // 16, nn % 16
                        P.dma("gpsimd", wm_st[bch % 2][:, bass.ts(kk, 512)],
                              wmod_i[bass.ts(i2, 128), bass.ts(og2, 512)],
                              dma[f"wm{bch % 2}"], deps=ldeps)
                mdeps = [(dma[f"wm{bch % 2}"].sem, 64 * (bch // 2 + 1), dma[f"wm{bch % 2}"].name)]
                if i == 0:
                    mdeps.append((cS.sem, k_sil, cS.name))
                    if og >= 4:
                        mdeps.append((cS.sem, evdone[og - 4], cS.name))
                need_inc = (k == 3) or (i == 15)
                P.emit("tensor",
                       (lambda og, i, bch, k: lambda t: t.matmul(
                           psb[og % 4][0:1, :], c_sil[:, i:i + 1],
                           wm_st[bch % 2][:, bass.ts(k, 512)],
                           start=(i == 0), stop=(i == 15)))(og, i, bch, k),
                       deps=mdeps, inc=cT if need_inc else None)
                if need_inc:
                    consumed_at[bch] = cT.n
            k_og = cT.n
            evdone.append(P.emit("scalar", lambda s, og=og: s.activation(
                modrow[:1, bass.ts(og, 512)], psb[og % 4][0:1, :], AF.Copy),
                deps=[(cT.sem, k_og, cT.name)]))
        k_modev = evdone[-1]

        P.emit("vector", lambda v: v.tensor_scalar(gtmp[:1, 0:D], modrow[:1, 2 * D:3 * D], -1.0, None, ALU.mult),
               deps=[(cS.sem, k_modev, cS.name)])
        P.emit("vector", lambda v: v.tensor_scalar(gtmp[:1, D:2 * D], modrow[:1, 5 * D:6 * D], -1.0, None, ALU.mult))
        P.emit("vector", lambda v: v.tensor_tensor(gtmp[:1, 0:D], gtmp[:1, 0:D], modrow[:1, 2 * D:3 * D], ALU.max))
        P.emit("vector", lambda v: v.tensor_tensor(gtmp[:1, D:2 * D], gtmp[:1, D:2 * D], modrow[:1, 5 * D:6 * D], ALU.max))
        P.emit("vector", lambda v: v.tensor_scalar_max(gtmp[:1, 0:D], gtmp[:1, 0:D], 0.25))
        P.emit("vector", lambda v: v.tensor_scalar_max(gtmp[:1, D:2 * D], gtmp[:1, D:2 * D], 0.25))
        k_abs = cV.n
        P.emit("scalar", lambda s: s.sign(gtmp2[:1, 0:D], modrow[:1, 2 * D:3 * D]),
               deps=[(cS.sem, k_modev, cS.name)])
        k_sign = P.emit("scalar", lambda s: s.sign(gtmp2[:1, D:2 * D], modrow[:1, 5 * D:6 * D]))
        P.emit("vector", lambda v: v.tensor_mul(modrow[:1, 2 * D:3 * D], gtmp[:1, 0:D], gtmp2[:1, 0:D]),
               deps=[(cS.sem, k_sign, cS.name)])
        P.emit("vector", lambda v: v.tensor_mul(modrow[:1, 5 * D:6 * D], gtmp[:1, D:2 * D], gtmp2[:1, D:2 * D]))
        k_modbf = P.emit("vector", lambda v: v.tensor_copy(modrow_bf[:1, :], modrow[:1, :]))

        BCAST = [(0, saF, True, False), (1, ca1F, True, True), (2, gaF, False, False),
                 (3, sbF, False, False), (4, cb1F, False, True), (5, gbF, False, False)]
        bc_ev = []
        for bi, (vi, dst, _early, plus1) in enumerate(BCAST):
            for j in range(4):
                idx = bi * 4 + j
                mdeps = [(cV.sem, k_modbf, cV.name)]
                if idx >= 4:
                    mdeps.append((cS.sem, bc_ev[idx - 4], cS.name))
                kmm = P.emit("tensor", (lambda vi, j: lambda t: t.matmul(
                    psb[j][:, :], ones1[:1, :],
                    modrow_bf[:1, vi * D + 512 * j: vi * D + 512 * (j + 1)],
                    start=True, stop=True))(vi, j), deps=mdeps, inc=cT)
                if plus1:
                    bc_ev.append(P.emit("scalar", (lambda dst, j: lambda s: s.activation(
                        dst[:, bass.ts(j, 512)], psb[j][:, :],
                        AF.Identity, bias=onecol[:, 0:1], scale=1.0))(dst, j),
                        deps=[(cT.sem, kmm, cT.name), (cV.sem, k_consts, cV.name)]))
                else:
                    bc_ev.append(P.emit("scalar", (lambda dst, j: lambda s: s.activation(
                        dst[:, bass.ts(j, 512)], psb[j][:, :], AF.Copy))(dst, j),
                        deps=[(cT.sem, kmm, cT.name)]))
        k_bcast = cS.n

        v_xread, s_xread, h_dma = [], [], []
        for t in range(NT):
            ldeps = []
            if t >= 2:
                ldeps += [(cV.sem, v_xread[t - 2], cV.name), (cS.sem, s_xread[t - 2], cS.name)]
            d_x = P.dma("gpsimd", x_st[t % 2][:, :], x_sl[bass.ts(t, 128), :],
                        dma[f"x{t % 2}"], deps=ldeps)
            P.emit("vector", lambda v, t=t: v.reduce_sum(stm[:, 0:1], x_st[t % 2][:, :], axis=AX),
                   deps=[d_x])
            k_mean = P.emit("vector", lambda v: v.tensor_scalar(stm[:, 2:3], stm[:, 0:1], RD, None, ALU.mult))
            k_sq = P.emit("scalar", lambda s, t=t: s.activation(
                sqscr[:, :], x_st[t % 2][:, :], AF.Square, accum_out=stm[:, 1:2]),
                deps=[d_x, (cV.sem, k_mean, cV.name)])
            P.emit("vector", lambda v: v.tensor_scalar(stm[:, 7:8], stm[:, 1:2], RD, None, ALU.mult),
                   deps=[(cS.sem, k_sq, cS.name)])
            P.emit("vector", lambda v: v.tensor_mul(stm[:, 3:4], stm[:, 2:3], stm[:, 2:3]))
            k_var = P.emit("vector", lambda v: v.tensor_sub(stm[:, 3:4], stm[:, 7:8], stm[:, 3:4]))
            k_std = P.emit("scalar", lambda s: s.activation(stm[:, 4:5], stm[:, 3:4], AF.Sqrt, bias=epscol[:, 0:1]),
                           deps=[(cV.sem, k_var, cV.name)])
            P.emit("vector", lambda v: v.reciprocal(stm[:, 5:6], stm[:, 4:5]),
                   deps=[(cS.sem, k_std, cS.name)])
            P.emit("vector", lambda v: v.tensor_mul(stm[:, 7:8], stm[:, 2:3], stm[:, 5:6]))
            k_inv = P.emit("vector", lambda v: v.tensor_scalar(stm[:, 6:7], stm[:, 7:8], -1.0, None, ALU.mult))
            xdeps = [(cV.sem, k_inv, cV.name)]
            if t >= 2:
                xdeps.append((cV.sem, v_xread[t - 2], cV.name))
            k_xh = P.emit("scalar", lambda s, t=t: s.activation(
                xh_t[t % 2][:, :], x_st[t % 2][:, :], AF.Identity,
                bias=stm[:, 6:7], scale=stm[:, 5:6]), deps=xdeps)
            s_xread.append(k_xh)
            hdeps = [(cS.sem, k_xh, cS.name), (cS.sem, k_bcast, cS.name)]
            if t >= 2:
                hdeps.append(h_dma[t - 2])
            P.emit("vector", lambda v, t=t: v.tensor_mul(h_t[t % 2][:, :], xh_t[t % 2][:, :], ca1F[:, :]),
                   deps=hdeps)
            v_xread.append(cV.n)
            k_h = P.emit("vector", lambda v, t=t: v.tensor_add(h_t[t % 2][:, :], h_t[t % 2][:, :], saF[:, :]))
            d_h = P.dma("sync", h_s[bass.ts(t, 128), :], h_t[t % 2][:, :], dma[f"h{t % 2}"],
                        deps=[(cV.sem, k_h, cV.name)])
            if "d_h" in dbg:
                d_h = P.dma("sync", dbg["d_h"][bass.ts(t, 128), :], h_t[t % 2][:, :], dma[f"h{t % 2}"])
            h_dma.append(d_h)

        if "d_mod" in dbg:
            P.dma("sync", dbg["d_mod"], modrow[:1, :], dma["dbg"],
                  deps=[(cV.sem, k_modbf, cV.name)])

        es01.close()

        esA = contextlib.ExitStack()
        TA = lambda name, shape, dt: esA.enter_context(nc.sbuf_tensor(name, shape, dt))
        hfm_st = [TA("hfm0", [128, 16 * 512], BF16)]
        wqa_st = TA("wqa_st", [128, 2048], BF16)
        wqb_st = TA("wqb_st", [128, 2048], BF16)
        wka_st = TA("wka_st", [128, 2048], BF16)
        wkb_st = TA("wkb_st", [128, 2048], BF16)
        wv_st = TA("wv_st", [128, 4096], BF16)
        w_st = [wqa_st, wqb_st, wka_st, wkb_st]
        preA = TA("preA", [128, TT], BF16)
        preB = TA("preB", [128, TT], BF16)
        preKA = TA("preKA", [128, TT], BF16)
        preKB = TA("preKB", [128, TT], BF16)
        rt1q = TA("rt1q", [128, TT], BF16)
        rt2q = TA("rt2q", [128, TT], BF16)
        rt1k = TA("rt1k", [128, TT], BF16)
        rt2k = TA("rt2k", [128, TT], BF16)
        ck_sb = TA("ck_sb", [128, TT], BF16)
        sk_sb = TA("sk_sb", [128, TT], BF16)
        qa_h = [TA(f"qa_h{k}", [32, TT], BF16) for k in range(2)]
        qb_h = [TA(f"qb_h{k}", [32, TT], BF16) for k in range(2)]
        ka_h = [TA(f"ka_h{k}", [32, TT], BF16) for k in range(2)]
        kb_h = [TA(f"kb_h{k}", [32, TT], BF16) for k in range(2)]
        v_buf = TA("v_buf", [128, NT * HPG * 65], BF16)
        sA_sb = TA("sA_sb", [32, HPG * S], BF16)
        sB_sb = TA("sB_sb", [32, HPG * S], BF16)
        sV_sb = TA("sV_sb", [16, HPG * 65], BF16)
        p_h = TA("p_h", [64, NB * 192], BF16)
        psink = TA("psink", [16, TT], BF16)
        stk = TA("stk", [128, 17 * 256], BF16)
        rcp = TA("rcp", [128, 1], F32)
        y_tok = TA("y_tok", [128, NTO * HPG * 64], BF16)
        m_all_sb = TA("m_all_sb", [64, NB * 192], BF16)
        bq_sb = TA("bq_sb", [128, 16], F32)
        bk_sb = TA("bk_sb", [128, 16], F32)
        bv_bf = TA("bv_bf", [1, D], BF16)
        kill0_sb = TA("kill0_sb", [64, 1], F32)
        kill33_sb = TA("kill33_sb", [64, 1], F32)

        ov_deps = [dma[nm].dep() for nm in ("c", "wm0", "wm1", "x0", "x1", "h0", "h1", "dbg")] + [
            (cS.sem, cS.n, cS.name), (cV.sem, cV.n, cV.name), (cT.sem, cT.n, cT.name)]
        P.dma("gpsimd", ck_sb[:, :], ck_i, dma["cst"], deps=ov_deps)
        P.dma("gpsimd", sk_sb[:, :], sk_i, dma["cst"])
        P.dma("gpsimd", m_all_sb[:, :], m_all_i, dma["cst"])
        P.dma("gpsimd", id_sb[:, :], ident_i, dma["cst"])
        P.dma("gpsimd", bq_sb[:, :], bq_i.rearrange("c p -> p c"), dma["cst"],
              allow_slow_non_contiguous=True)
        P.dma("gpsimd", bk_sb[:, :], bk_i.rearrange("c p -> p c"), dma["cst"],
              allow_slow_non_contiguous=True)
        P.dma("gpsimd", bv_bf[:1, :], bv_i, dma["cst"])
        P.dma("gpsimd", kill0_sb[:, :], kill0_i, dma["cst"])
        d_cst = P.dma("gpsimd", kill33_sb[:, :], kill33_i, dma["cst"])
        P.emit("vector", lambda v: v.memset(
            bass.AP(v_buf, 64, [[NT * HPG * 65, 128], [65, NT * HPG], [1, 1]]), 1.0),
            deps=ov_deps)
        P.emit("vector", lambda v: v.memset(
            bass.AP(sV_sb, 64, [[HPG * 65, 16], [65, HPG], [1, 1]]), 1.0))
        P.emit("vector", lambda v: v.memset(stk[:, :], 0.0))
        k_init = P.emit("vector", lambda v: v.memset(p_h[0:64, :], 0.0))

        TGS = [(0, 512), (512, 512), (1024, 512), (1536, 512), (2048, 128)]
        ev_qk = {}
        ev_v = {}
        hfm_free = [0, 0]
        wset_used = 0
        sslot_free = [0, 0, 0, 0]
        sink_ev_last = 0
        psink_free = 0
        ph_free = 0
        stk_free = 0
        rcp_free = 0
        pv_slot_free = [0, 0]
        ytok_free = 0
        yt_dma_prev = None
        dbg_pend = None
        last_att = 0

        for hg in range(NHG):
            wdeps = ([(cT.sem, wset_used, cT.name)] if hg else []) + (
                [dbg_pend] if dbg_pend else [])
            for i in range(16):
                P.dma("gpsimd", wqa_st[:, bass.ts(i, 128)],
                      WqT_i[bass.ts(i, 128), 256 * hg: 256 * hg + 128], dma["ws"], deps=wdeps)
                P.dma("gpsimd", wqb_st[:, bass.ts(i, 128)],
                      WqT_i[bass.ts(i, 128), 256 * hg + 128: 256 * hg + 256], dma["ws"], deps=wdeps)
                P.dma("gpsimd", wka_st[:, bass.ts(i, 128)],
                      WkT_i[bass.ts(i, 128), 256 * hg: 256 * hg + 128], dma["ws"], deps=wdeps)
                P.dma("gpsimd", wkb_st[:, bass.ts(i, 128)],
                      WkT_i[bass.ts(i, 128), 256 * hg + 128: 256 * hg + 256], dma["ws"], deps=wdeps)
                P.dma("gpsimd", wv_st[:, bass.ts(i, 256)],
                      WvT_i[bass.ts(i, 128), 256 * hg: 256 * hg + 256], dma["ws"], deps=wdeps)
            d_ws = dma["ws"].dep()
            sdeps = [(cT.sem, last_att, cT.name)] if hg else []
            for hh in range(HPG):
                h = hg * HPG + hh
                P.dma("gpsimd", sA_sb[0:32, bass.ts(hh, S)], sinksA_i[h], dma["snk"], deps=sdeps)
                P.dma("gpsimd", sB_sb[0:32, bass.ts(hh, S)], sinksB_i[h], dma["snk"], deps=sdeps)
                P.dma("gpsimd", sV_sb[0:16, 65 * hh: 65 * hh + 64], sinksV_i[h], dma["snk"], deps=sdeps)
            d_snk = dma["snk"].dep()

            for tg, (tgo, tgw) in enumerate(TGS):
                lastt = (tgo + tgw - 1) // 128
                hdeps = [h_dma[lastt], (cT.sem, hfm_free[0], cT.name)] + (
                    ov_deps if (hg, tg) == (0, 0) else [])
                for i in range(16):
                    P.dma("sync", hfm_st[0][:, 512 * i: 512 * i + tgw],
                          h_s[tgo:tgo + tgw, bass.ts(i, 128)], dma["hf0"],
                          deps=hdeps, transpose=True)
                d_hf = dma["hf0"].dep()
                for oc in range(4):
                    mdeps = [d_hf, d_ws]
                    prev = ev_qk.get((tg - 1 if tg else 4, oc))
                    if prev:
                        mdeps.append((cS.sem, prev, cS.name))
                    for i in range(16):
                        P.emit("tensor", (lambda oc, i, tg, tgw: lambda t: t.matmul(
                            psb[oc][:, 0:tgw],
                            w_st[oc][:, bass.ts(i, 128)],
                            hfm_st[0][:, 512 * i: 512 * i + tgw],
                            start=(i == 0), stop=(i == 15)))(oc, i, tg, tgw),
                            deps=mdeps if i == 0 else (), inc=cT if i == 15 else None)
                    k_mm = cT.n
                    pre_t = [preA, preB, preKA, preKB][oc]
                    bias = [bq_sb[:, 2 * hg:2 * hg + 1], bq_sb[:, 2 * hg + 1:2 * hg + 2],
                            bk_sb[:, 2 * hg:2 * hg + 1], bk_sb[:, 2 * hg + 1:2 * hg + 2]][oc]
                    ev_qk[(tg, oc)] = P.emit("scalar", (lambda pre_t, bias, tgo, tgw, oc: lambda s: s.activation(
                        pre_t[:, tgo:tgo + tgw], psb[oc][:, 0:tgw],
                        AF.Identity, bias=bias, scale=1.0))(pre_t, bias, tgo, tgw, oc),
                        deps=[(cT.sem, k_mm, cT.name), d_cst])
                for tt in range(tgw // 128):
                    t_glob = tgo // 128 + tt
                    sl = t_glob % 3
                    vdeps = [d_hf, d_ws, d_cst, (cV.sem, k_init, cV.name)] + (
                        [dbg_pend] if dbg_pend else [])
                    if ev_v.get(sl):
                        vdeps.append((cS.sem, ev_v[sl], cS.name))
                    P.emit("tensor", (lambda sl, hg: lambda t: t.matmul(
                        psb[4 + sl][:, 0:256],
                        ones1[:1, :], bv_bf[:1, 256 * hg: 256 * hg + 256],
                        start=True, stop=False))(sl, hg), deps=vdeps)
                    for i in range(16):
                        P.emit("tensor", (lambda sl, i, tg, tt: lambda t: t.matmul(
                            psb[4 + sl][:, 0:256],
                            hfm_st[0][:, 512 * i + 128 * tt: 512 * i + 128 * tt + 128],
                            wv_st[:, bass.ts(i, 256)],
                            start=False, stop=(i == 15)))(sl, i, tg, tt),
                            inc=cT if i == 15 else None)
                    k_vmm = cT.n
                    ev_v[sl] = P.emit("scalar", (lambda t_glob, sl: lambda s: s.activation(
                        v_buf[:, 260 * t_glob: 260 * (t_glob + 1)].rearrange(
                            "p (h c) -> p h c", c=65)[:, :, 0:64],
                        psb[4 + sl][:, 0:256].rearrange("p (h c) -> p h c", c=64),
                        AF.Copy))(t_glob, sl),
                        deps=[(cT.sem, k_vmm, cT.name)])
                hfm_free[0] = cT.n
            wset_used = cT.n
            k_qkev = max(ev_qk[(4, oc)] for oc in range(4))
            k_vev = max(ev_v.values())

            rdeps = [(cS.sem, k_qkev, cS.name), d_cst, (cT.sem, last_att, cT.name)]
            P.emit("vector", lambda v: v.tensor_mul(rt1q[:, :], preA[:, :], sk_sb[:, :]), deps=rdeps)
            P.emit("vector", lambda v: v.tensor_mul(rt2q[:, :], preB[:, :], sk_sb[:, :]))
            P.emit("vector", lambda v: v.tensor_mul(preA[:, :], preA[:, :], ck_sb[:, :]))
            P.emit("vector", lambda v: v.tensor_mul(preB[:, :], preB[:, :], ck_sb[:, :]))
            P.emit("vector", lambda v: v.tensor_mul(rt1k[:, :], preKA[:, :], sk_sb[:, :]))
            P.emit("vector", lambda v: v.tensor_mul(rt2k[:, :], preKB[:, :], sk_sb[:, :]))
            P.emit("vector", lambda v: v.tensor_mul(preKA[:, :], preKA[:, :], ck_sb[:, :]))
            P.emit("vector", lambda v: v.tensor_mul(preKB[:, :], preKB[:, :], ck_sb[:, :]))
            k_ropep = cV.n

            qkh_free = [0, 0]
            for hh in range(HPG):
                sl2 = hh % 2
                fdeps = [(cV.sem, k_ropep, cV.name), (cT.sem, qkh_free[sl2], cT.name)]
                P.emit("vector", (lambda hh, sl2: lambda v: v.tensor_sub(
                    qa_h[sl2][0:32, :], preA[32 * hh:32 * hh + 32, :], rt2q[32 * hh:32 * hh + 32, :]))(hh, sl2),
                    deps=fdeps)
                P.emit("vector", (lambda hh, sl2: lambda v: v.tensor_add(
                    qb_h[sl2][0:32, :], rt1q[32 * hh:32 * hh + 32, :], preB[32 * hh:32 * hh + 32, :]))(hh, sl2))
                P.emit("vector", (lambda hh, sl2: lambda v: v.tensor_sub(
                    ka_h[sl2][0:32, :], preKA[32 * hh:32 * hh + 32, :], rt2k[32 * hh:32 * hh + 32, :]))(hh, sl2))
                k_rope = P.emit("vector", (lambda hh, sl2: lambda v: v.tensor_add(
                    kb_h[sl2][0:32, :], rt1k[32 * hh:32 * hh + 32, :], preKB[32 * hh:32 * hh + 32, :]))(hh, sl2))
                for qg, (qgo, qgw) in enumerate(TGS):
                    sdeps2 = [(cV.sem, k_rope, cV.name), d_snk,
                              (cS.sem, sink_ev_last, cS.name),
                              (cT.sem, psink_free, cT.name)]
                    P.emit("tensor", (lambda hh, qgo, qgw: lambda t: t.matmul(
                        psb[4][0:16, 0:qgw],
                        sA_sb[0:32, bass.ts(hh, S)], qa_h[hh % 2][0:32, qgo:qgo + qgw],
                        start=True, stop=False))(hh, qgo, qgw), deps=sdeps2)
                    P.emit("tensor", (lambda hh, qgo, qgw: lambda t: t.matmul(
                        psb[4][0:16, 0:qgw],
                        sB_sb[0:32, bass.ts(hh, S)], qb_h[hh % 2][0:32, qgo:qgo + qgw],
                        start=False, stop=True))(hh, qgo, qgw), inc=cT)
                    k_smm = cT.n
                    sink_ev_last = P.emit("scalar", (lambda qgo, qgw: lambda s: s.activation(
                        psink[0:16, qgo:qgo + qgw], psb[4][0:16, 0:qgw],
                        AF.Exp, scale=0.125))(qgo, qgw), deps=[(cT.sem, k_smm, cT.name)])
                exp_local = {}
                for m in range(NB):
                    so = 64 * max(m - 1, 0)
                    spw = 64 * (min(m + 1, NB - 1) - max(m - 1, 0) + 1)

                    bdeps = [(cV.sem, k_rope, cV.name),
                             (cS.sem, sslot_free[m % 4], cS.name),
                             (cV.sem, ph_free, cV.name)]
                    P.emit("tensor", (lambda hh, m, so, spw: lambda t: t.matmul(
                        psb[m % 4][0:64, 0:spw],
                        ka_h[hh % 2][0:32, 64 * m:64 * m + 64], qa_h[hh % 2][0:32, so:so + spw],
                        start=True, stop=False))(hh, m, so, spw), deps=bdeps)
                    P.emit("tensor", (lambda hh, m, so, spw: lambda t: t.matmul(
                        psb[m % 4][0:64, 0:spw],
                        kb_h[hh % 2][0:32, 64 * m:64 * m + 64], qb_h[hh % 2][0:32, so:so + spw],
                        start=False, stop=True))(hh, m, so, spw), inc=cT)
                    k_bmm = cT.n
                    sslot_free[m % 4] = exp_local[m] = P.emit(
                        "scalar", (lambda m, spw: lambda s: s.activation(
                            p_h[0:64, 192 * m: 192 * m + spw], psb[m % 4][0:64, 0:spw],
                            AF.Exp, scale=0.125))(m, spw),
                        deps=[(cT.sem, k_bmm, cT.name), (cV.sem, ph_free, cV.name)])
                k_exp = cS.n
                P.emit("vector", lambda v: v.tensor_mul(p_h[0:64, :], p_h[0:64, :], m_all_sb[0:64, :]),
                       deps=[(cS.sem, k_exp, cS.name), d_cst])
                P.emit("vector", lambda v: v.tensor_scalar(p_h[0:64, 0:128], p_h[0:64, 0:128],
                                                           kill0_sb[0:64, 0:1], None, ALU.mult))
                k_msk = P.emit("vector", lambda v: v.tensor_scalar(
                    p_h[0:64, 192 * 33:192 * 33 + 128], p_h[0:64, 192 * 33:192 * 33 + 128],
                    kill33_sb[0:64, 0:1], None, ALU.mult))
                stk_deps = [(cV.sem, k_msk, cV.name), (cT.sem, stk_free, cT.name)]
                P.emit("vector", lambda v: v.tensor_copy(stk[0:64, 64:192], p_h[0:64, 0:128]),
                       deps=stk_deps)
                P.emit("vector", lambda v: v.tensor_copy(
                    stk[0:64, 256:256 + 16 * 256].rearrange("p (j c) -> p j c", c=256)[:, :, 0:192],
                    p_h[0:64, 384:384 + 16 * 384].rearrange("p (u c) -> p u c", c=384)[:, :, 0:192]))
                P.emit("vector", lambda v: v.tensor_copy(
                    stk[64:128, 64:64 + 16 * 256].rearrange("p (j c) -> p j c", c=256)[:, :, 0:192],
                    p_h[0:64, 192:192 + 16 * 384].rearrange("p (u c) -> p u c", c=384)[:, :, 0:192]))
                k_stk = P.emit("vector", lambda v: v.tensor_copy(
                    stk[64:128, 256 * 16 + 64: 256 * 16 + 192], p_h[0:64, 192 * 33:192 * 33 + 128]))
                ph_free = k_stk
                for i in range(1, 17):
                    pst = psb[5 + (i % 2)]
                    vdeps2 = [(cV.sem, k_stk, cV.name), (cS.sem, sink_ev_last, cS.name),
                              (cS.sem, k_vev, cS.name),
                              (cS.sem, pv_slot_free[i % 2], cS.name)] + (
                                  [yt_dma_prev] if yt_dma_prev is not None else [])
                    P.emit("tensor", (lambda i, hh, pst: lambda t: t.matmul(
                        pst[:, 0:65],
                        stk[:, 256 * (i - 1) + 128: 256 * (i - 1) + 256],
                        v_buf[:, 260 * (i - 1) + 65 * hh: 260 * (i - 1) + 65 * hh + 65],
                        start=True, stop=False))(i, hh, pst), deps=vdeps2)
                    P.emit("tensor", (lambda i, hh, pst: lambda t: t.matmul(
                        pst[:, 0:65],
                        stk[:, 256 * i: 256 * i + 128],
                        v_buf[:, 260 * i + 65 * hh: 260 * i + 65 * hh + 65],
                        start=False, stop=False))(i, hh, pst))
                    P.emit("tensor", (lambda i, hh, pst: lambda t: t.matmul(
                        pst[:, 0:65],
                        psink[0:16, 64 * (2 * i - 1): 64 * (2 * i - 1) + 128],
                        sV_sb[0:16, 65 * hh: 65 * hh + 65],
                        start=False, stop=True))(i, hh, pst), inc=cT)
                    k_pv = cT.n
                    P.emit("vector", (lambda pst: lambda v: v.reciprocal(
                        rcp[:, 0:1], pst[:, 64:65]))(pst),
                        deps=[(cT.sem, k_pv, cT.name), (cS.sem, rcp_free, cS.name)])
                    k_rcp = cV.n
                    rcp_free = pv_slot_free[i % 2] = P.emit(
                        "scalar", (lambda i, hh, pst: lambda s: s.activation(
                            y_tok[:, 256 * (i - 1) + 64 * hh: 256 * (i - 1) + 64 * hh + 64],
                            pst[:, 0:64], AF.Identity,
                            bias=0.0, scale=rcp[:, 0:1]))(i, hh, pst),
                        deps=[(cV.sem, k_rcp, cV.name)])
                stk_free = psink_free = cT.n
                qkh_free[sl2] = cT.n
                if "d_ph" in dbg and hg == 0 and hh == 0:
                    P.dma("sync", dbg["d_ph"], p_h[0:64, :], dma["dbg"],
                          deps=[(cV.sem, k_stk, cV.name)])
                    P.dma("sync", dbg["d_ps"], psink[0:16, :], dma["dbg"],
                          deps=[(cS.sem, sink_ev_last, cS.name)])
                    P.dma("sync", dbg["d_qa"], qa_h[0][0:32, :], dma["dbg"])
                    P.dma("sync", dbg["d_qb"], qb_h[0][0:32, :], dma["dbg"])
                    P.dma("sync", dbg["d_ka"], ka_h[0][0:32, :], dma["dbg"])
                    P.dma("sync", dbg["d_kb"], kb_h[0][0:32, :], dma["dbg"])
                    P.dma("sync", dbg["d_stk"], stk[:, :], dma["dbg"])
                    P.dma("sync", dbg["d_vb"], v_buf[:, :], dma["dbg"],
                          deps=[(cS.sem, k_vev, cS.name)])
                    P.emit("vector", lambda v: v.memset(stm[:, 7:8], 0.0),
                           deps=[dma["dbg"].dep()])
                    dbg_pend = dma["dbg"].dep()
            last_att = cT.n
            k_pvall = cS.n
            if "d_yt" in dbg and hg == 0:
                P.dma("sync", dbg["d_yt"], y_tok[:, :], dma["dbg"],
                      deps=[(cS.sem, cS.n, cS.name)])
                dbg_pend = dma["dbg"].dep()
            yt_dma_prev = P.dma(
                "sync",
                y_s.rearrange("(a p) c -> p a c", p=128)[:, :, 256 * hg: 256 * hg + 256],
                y_tok[:, :].rearrange("p (a c) -> p a c", c=256),
                dma["yt"], deps=[(cS.sem, k_pvall, cS.name)])
            ytok_free = 0
        k_att_T = cT.n
        d_yt_all = dma["yt"].dep()

        def emit_stub_out(scope_T, deps):
            stub = scope_T("stub_st", [128, D], F32)
            for t in range(NTO):
                d_m = P.dma("sync", stub[:, :], x_sl[128 * t + HALO:128 * (t + 1) + HALO, :],
                            dma["x2b0"], deps=deps + [dma["ob0"].dep()])
                P.dma("sync", out[bass.ts(t, 128), :], stub[:, :], dma["ob0"], deps=[d_m])
            fin0 = [dma["ob0"].dep(), dma["dbg"].dep()]
            P.emit("sync", lambda e: [e.wait_ge(sm_, v_) for sm_, v_, _ in fin0][-1])

        esA.close()
        esB = contextlib.ExitStack()
        TB = lambda name, shape, dt: esB.enter_context(nc.sbuf_tensor(name, shape, dt))
        if KPH < 1:
            emit_stub_out(TB, [d_yt_all, (cT.sem, cT.n, cT.name), (cS.sem, cS.n, cS.name),
                               (cV.sem, cV.n, cV.name)])
        yT_full = TB("yT_full", [128, 16 * TOK], BF16)
        wo_full = TB("wo_full", [128, 16 * D], BF16)
        x_st2 = [TB(f"x_st2{k}", [128, D], F32) for k in range(2)]
        scr8 = TB("scr8", [128, D], F32)
        attn_t = scr8
        sq2 = scr8
        rtmp = scr8
        _x2 = TB("x2_t0", [128, D], F32)
        x2_t = [_x2, _x2]
        xh2_t = TB("xh2_t", [128, D], BF16)
        _h2 = TB("h2_t0", [128, D], BF16)
        h2_t = [_h2, _h2]
        _xr1 = TB("xr1_t0", [128, D], BF16)
        xr1_t = [_xr1, _xr1]
        SQD = float(math.sqrt(D))
        SQI = float(math.sqrt(INNER))

        bo_bf = TB("bo_bf", [1, D], BF16)
        ovB = [dma[nm].dep() for nm in ("cst", "ws", "snk", "hf0", "yt", "dbg")] + [
            (cT.sem, cT.n, cT.name), (cS.sem, cS.n, cS.name), (cV.sem, cV.n, cV.name)]
        P.dma("gpsimd", bo_bf[:1, :], bo_i, dma["wol"], deps=ovB)
        for i in range(16):
            P.dma("sync", yT_full[:, bass.ts(i, TOK)], y_s[:, bass.ts(i, 128)],
                  dma["yld"], deps=ovB + [d_yt_all], transpose=True)
            P.dma("gpsimd", wo_full[:, bass.ts(i, D)], WoT_i[bass.ts(i, 128), :],
                  dma["wol"], deps=ovB)
        d_yld = dma["yld"].dep()
        d_wol = dma["wol"].dep()

        ev8 = {}
        v2_read = []
        h2_free = []
        x2cons = []
        for tt in range(NTO):
            xdeps = list(ovB) if tt < 2 else [(cV.sem, v2_read[tt - 2], cV.name)]
            d_x = P.dma("gpsimd", x_st2[tt % 2][:, :],
                        x_sl[128 * tt + HALO: 128 * (tt + 1) + HALO, :],
                        dma[f"xa{tt % 2}"], deps=xdeps)
            for oc in range(4):
                pdeps = [d_yld, d_wol, d_cst]
                if oc in ev8:
                    pdeps.append((cV.sem, ev8[oc], cV.name))
                P.emit("tensor", (lambda oc: lambda t: t.matmul(
                    psb[oc][:, :], ones1[:1, :], bo_bf[:1, bass.ts(oc, 512)],
                    start=True, stop=False))(oc), deps=pdeps)
            for j in range(16):
                for oc in range(4):
                    P.emit("tensor", (lambda j, oc, tt: lambda t: t.matmul(
                        psb[oc][:, :],
                        yT_full[:, TOK * j + 128 * tt: TOK * j + 128 * tt + 128],
                        wo_full[:, D * j + 512 * oc: D * j + 512 * (oc + 1)],
                        start=False, stop=(j == 15)))(j, oc, tt),
                        inc=cT if (j == 15 and oc == 3) else None)
            k_wo = cT.n
            for oc in range(4):
                P.emit("vector", (lambda oc: lambda v: v.tensor_mul(
                    attn_t[:, bass.ts(oc, 512)], psb[oc][:, :], gaF[:, bass.ts(oc, 512)]))(oc),
                    deps=[(cT.sem, k_wo, cT.name)])
                ev8[oc] = cV.n
            x2deps = [d_x] + ([dma["x2a0"].dep()] if tt >= 1 else []) + (
                [(cS.sem, x2cons[tt - 1], cS.name)] if tt >= 1 else [])
            for oc in range(4):
                P.emit("vector", (lambda oc, tt: lambda v: v.tensor_add(
                    x2_t[tt % 2][:, bass.ts(oc, 512)], x_st2[tt % 2][:, bass.ts(oc, 512)],
                    attn_t[:, bass.ts(oc, 512)]))(oc, tt), deps=x2deps)
            k_x2 = cV.n
            P.dma("sync", x2_s[bass.ts(tt, 128), :], x2_t[tt % 2][:, :],
                  dma["x2a0"], deps=[(cV.sem, k_x2, cV.name)])
            P.emit("vector", (lambda tt: lambda v: v.reduce_sum(st2[:, 0:1], x2_t[tt % 2][:, :], axis=AX))(tt))
            k_mean2 = P.emit("vector", lambda v: v.tensor_scalar(st2[:, 2:3], st2[:, 0:1], RD, None, ALU.mult))
            k_sq2 = P.emit("scalar", (lambda tt: lambda s: s.activation(
                sq2[:, :], x2_t[tt % 2][:, :], AF.Square, accum_out=st2[:, 1:2]))(tt),
                deps=[(cV.sem, k_mean2, cV.name)])
            P.emit("vector", lambda v: v.tensor_scalar(st2[:, 7:8], st2[:, 1:2], RD, None, ALU.mult),
                   deps=[(cS.sem, k_sq2, cS.name)])
            P.emit("vector", lambda v: v.tensor_mul(st2[:, 3:4], st2[:, 2:3], st2[:, 2:3]))
            k_var2 = P.emit("vector", lambda v: v.tensor_sub(st2[:, 3:4], st2[:, 7:8], st2[:, 3:4]))
            k_std2 = P.emit("scalar", lambda s: s.activation(st2[:, 4:5], st2[:, 3:4], AF.Sqrt,
                                                             bias=epscol[:, 0:1]),
                            deps=[(cV.sem, k_var2, cV.name)])
            P.emit("vector", lambda v: v.reciprocal(st2[:, 5:6], st2[:, 4:5]),
                   deps=[(cS.sem, k_std2, cS.name)])
            P.emit("vector", lambda v: v.tensor_mul(st2[:, 7:8], st2[:, 2:3], st2[:, 5:6]))
            k_inv2 = P.emit("vector", lambda v: v.tensor_scalar(st2[:, 6:7], st2[:, 7:8], -1.0, None, ALU.mult))
            k_xh2 = P.emit("scalar", (lambda tt: lambda s: s.activation(
                xh2_t[:, :], x2_t[tt % 2][:, :], AF.Identity,
                bias=st2[:, 6:7], scale=st2[:, 5:6]))(tt),
                deps=[(cV.sem, k_inv2, cV.name)])
            v2_read.append(cV.n)
            x2cons.append(k_xh2)
            hdeps2 = [(cS.sem, k_xh2, cS.name), (cS.sem, k_bcast, cS.name)]
            if tt >= 1:
                hdeps2 += h2_free[tt - 1]
            P.emit("vector", (lambda tt: lambda v: v.tensor_mul(h2_t[tt % 2][:, :], xh2_t[:, :], cb1F[:, :]))(tt),
                   deps=hdeps2)
            P.emit("vector", (lambda tt: lambda v: v.tensor_add(h2_t[tt % 2][:, :], h2_t[tt % 2][:, :], sbF[:, :]))(tt))
            P.emit("vector", (lambda tt: lambda v: v.reduce_max(st2[:, 8:9], h2_t[tt % 2][:, :], axis=AX,
                                                                apply_absolute_value=True))(tt))
            k_ss1 = P.emit("scalar", (lambda tt: lambda s: s.activation(
                sq2[:, :], h2_t[tt % 2][:, :], AF.Square, accum_out=st2[:, 9:10]))(tt),
                deps=[(cV.sem, cV.n, cV.name)])
            k_n1 = P.emit("scalar", lambda s: s.activation(st2[:, 10:11], st2[:, 9:10], AF.Sqrt),
                          deps=[(cS.sem, k_ss1, cS.name)])
            P.emit("vector", lambda v: v.tensor_scalar_max(st2[:, 10:11], st2[:, 10:11], 1e-12),
                   deps=[(cS.sem, k_n1, cS.name)])
            P.emit("vector", lambda v: v.reciprocal(st2[:, 11:12], st2[:, 10:11]))
            P.emit("vector", lambda v: v.tensor_mul(st2[:, 8:9], st2[:, 8:9], st2[:, 11:12]))
            P.emit("vector", lambda v: v.tensor_scalar(st2[:, 8:9], st2[:, 8:9], SQD, None, ALU.mult))
            P.emit("vector", lambda v: v.tensor_scalar_max(st2[:, 8:9], st2[:, 8:9], 1e-5))
            P.emit("vector", (lambda tt: lambda v: v.tensor_scalar(
                drs1_all[:, tt:tt + 1], st2[:, 8:9], 1.0 / 127.0, None, ALU.mult))(tt))
            P.emit("vector", lambda v: v.reciprocal(st2[:, 9:10], st2[:, 8:9]))
            P.emit("vector", lambda v: v.tensor_mul(st2[:, 9:10], st2[:, 9:10], st2[:, 11:12]))
            P.emit("vector", lambda v: v.tensor_scalar(st2[:, 9:10], st2[:, 9:10], 127.0 * SQD, None, ALU.mult))
            P.emit("vector", (lambda tt: lambda v: v.tensor_scalar(
                rtmp[:, :], h2_t[tt % 2][:, :], st2[:, 9:10], MAGIC, ALU.mult, ALU.add))(tt))
            xrdeps = [dma["xr10"].dep()] if tt >= 1 else []
            k_xr1 = P.emit("vector", (lambda tt: lambda v: v.tensor_scalar(
                xr1_t[tt % 2][:, :], rtmp[:, :], -MAGIC, None, ALU.add))(tt), deps=xrdeps)
            h2_free.append([(cV.sem, k_xr1, cV.name), (cS.sem, k_ss1, cS.name)])
            P.dma("sync", Xr1_s[bass.ts(tt, 128), :], xr1_t[tt % 2][:, :],
                  dma["xr10"], deps=[(cV.sem, k_xr1, cV.name)])
        d_xr1_all = [dma["xr10"].dep()]
        d_x2_all = [dma["x2a0"].dep()]

        esB.close()
        esC = contextlib.ExitStack()
        TC = lambda name, shape, dt: esC.enter_context(nc.sbuf_tensor(name, shape, dt))
        w_f32 = [TC(f"w_f32{k}", [128, INNER], F32) for k in range(2)]
        w_abs = TC("w_abs", [128, INNER], BF16)
        w_t1 = TC("w_t1", [128, INNER], F32)
        w_out = [TC(f"w_out{k}", [128, INNER], BF16) for k in range(2)]

        ovC = [dma[nm].dep() for nm in ("yld", "wol", "xa0", "xa1", "x2a0", "x2a1",
                                        "xr10", "xr11")] + [
            (cT.sem, cT.n, cT.name), (cS.sem, cS.n, cS.name), (cV.sem, cV.n, cV.name)]
        cG = CH["gpsimd"]

        def quant_weight(w_in, w_rs, ntile, FDW, col, start_deps):
            absk, addk, wrk = [], [], []
            for ti in range(ntile):
                ld = list(start_deps) if ti < 2 else [(cS.sem, absk[ti - 2], cS.name)]
                d_w = P.dma("gpsimd", w_f32[ti % 2][:, 0:FDW], w_in[bass.ts(ti, 128), :],
                            dma[f"wfa{ti % 2}"], deps=ld)
                adep = [d_w]
                if ti >= 8:
                    adep.append((cV.sem, addk[ti - 8], cV.name))
                absk.append(P.emit("scalar", (lambda ti, FDW: lambda s: s.activation(
                    w_abs[:, 0:FDW], w_f32[ti % 2][:, 0:FDW], AF.Abs,
                    accum_out=rs_c[:, 2 + ti % 8: 3 + ti % 8]))(ti, FDW), deps=adep))
                if ti == 0:
                    addk.append(P.emit("vector", lambda v: v.tensor_copy(rs_c[:, 0:1], rs_c[:, 2:3]),
                                       deps=[(cS.sem, absk[-1], cS.name)]))
                else:
                    addk.append(P.emit("vector", (lambda ti: lambda v: v.tensor_add(
                        rs_c[:, 0:1], rs_c[:, 0:1], rs_c[:, 2 + ti % 8: 3 + ti % 8]))(ti),
                        deps=[(cS.sem, absk[-1], cS.name)]))
            d1 = P.dma("sync", rs_s[0:1, :], rs_c[:, 0:1],
                       dma["rsa"], deps=[(cV.sem, addk[-1], cV.name)],
                       allow_slow_non_contiguous=True)
            d2 = P.dma("gpsimd", w_abs[:, 0:256].bitcast(F32)[0:1, 0:128],
                       rs_s[0:1, :], dma["rsg"], deps=[d1])
            k_tot = P.emit("vector", lambda v: v.reduce_sum(
                rs_c[0:1, 1:2], w_abs[:, 0:256].bitcast(F32)[0:1, 0:128], axis=AX), deps=[d2])
            d3 = P.dma("sync", rs_s[1:2, 0:1], rs_c[0:1, 1:2], dma["rsa"],
                       deps=[(cV.sem, k_tot, cV.name)])
            d4 = P.dma("gpsimd", rs_c[:, 1:2], rs_s[1:2, 0:1].to_broadcast([128, 1]),
                       dma["rsg"], deps=[d3])
            NW = 128.0 * ntile * FDW
            P.emit("vector", (lambda col, NW: lambda v: v.tensor_scalar(
                wsc_c[:, col + 1:col + 2], rs_c[:, 1:2], 1.0 / NW, None, ALU.mult))(col, NW),
                deps=[d4])
            P.emit("vector", (lambda col: lambda v: v.tensor_scalar_max(
                wsc_c[:, col + 1:col + 2], wsc_c[:, col + 1:col + 2], 1e-5))(col))
            k_wm = P.emit("vector", (lambda col: lambda v: v.reciprocal(
                wsc_c[:, col:col + 1], wsc_c[:, col + 1:col + 2]))(col))
            for ti in range(ntile):
                ld = [(cV.sem, k_wm, cV.name)]
                if ti >= 2:
                    ld.append((cV.sem, wrk[ti - 2], cV.name))
                d_w = P.dma("gpsimd", w_f32[ti % 2][:, 0:FDW], w_in[bass.ts(ti, 128), :],
                            dma[f"wfa{ti % 2}"], deps=ld)
                P.emit("vector", (lambda ti, col, FDW: lambda v: v.tensor_scalar(
                    w_t1[:, 0:FDW], w_f32[ti % 2][:, 0:FDW], wsc_c[:, col:col + 1], MAGIC,
                    ALU.mult, ALU.add))(ti, col, FDW), deps=[d_w])
                P.emit("vector", (lambda FDW: lambda v: v.tensor_scalar(
                    w_t1[:, 0:FDW], w_t1[:, 0:FDW], -MAGIC, 1.0, ALU.add, ALU.min))(FDW))
                stdep = [dma[f"wr{ti % 2}"].dep()]
                wrk.append(P.emit("vector", (lambda ti, FDW: lambda v: v.tensor_scalar(
                    w_out[ti % 2][:, 0:FDW], w_t1[:, 0:FDW], -1.0, None, ALU.max))(ti, FDW),
                    deps=stdep))
                P.dma("sync", w_rs[bass.ts(ti, 128), :], w_out[ti % 2][:, 0:FDW],
                      dma[f"wr{ti % 2}"], deps=[(cV.sem, wrk[-1], cV.name)])
            return wrk[-1]

        k_w1q = quant_weight(w1T_i, w1r_s, 16, INNER, 0, ovC)
        k_w2q = quant_weight(w2T_i, w2r_s, 64, D, 2,
                             [(cV.sem, k_w1q, cV.name)])
        k_s1 = P.emit("vector", lambda v: v.tensor_scalar(
            s1_all[:, :], drs1_all[:, :], wsc_c[:, 1:2], None, ALU.mult))
        d_wr_all = [dma["wr0"].dep(), dma["wr1"].dep()]

        esC.close()
        esD = contextlib.ExitStack()
        TD = lambda name, shape, dt: esD.enter_context(nc.sbuf_tensor(name, shape, dt))
        xr1T = TD("xr1T", [128, 16 * TOK], BF16)
        w1q_st = [TD(f"w1q_st{k}", [128, 16 * 512], BF16) for k in range(2)]
        g1_t = TD("g1_t", [128, 512], F32)
        u2_t = [TD(f"u2_t{k}", [128, 512], BF16) for k in range(2)]

        ovD = [dma[nm].dep() for nm in ("wfa0", "wfa1", "wr0", "wr1")] + [
            (cT.sem, cT.n, cT.name), (cS.sem, cS.n, cS.name), (cV.sem, cV.n, cV.name)]
        for i in range(16):
            P.dma("sync", xr1T[:, bass.ts(i, TOK)], Xr1_s[:, bass.ts(i, 128)],
                  dma["x1T"], deps=ovD + d_xr1_all, transpose=True)
        d_x1T = dma["x1T"].dep()

        u2k = {}
        ev10s = {}
        for og16 in range(16):
            wdep = list(ovD) if og16 < 2 else [(cT.sem, u2k[og16 - 2], cT.name)]
            for i in range(16):
                P.dma("gpsimd", w1q_st[og16 % 2][:, bass.ts(i, 512)],
                      w1r_s[bass.ts(i, 128), bass.ts(og16, 512)],
                      dma[f"w1a{og16 % 2}"], deps=wdep + d_wr_all)
            d_w1 = dma[f"w1a{og16 % 2}"].dep()
            for tt in range(NTO):
                mdeps = [d_x1T, d_w1]
                sl = tt % 2
                if sl in ev10s:
                    mdeps.append((cS.sem, ev10s[sl], cS.name))
                for i in range(16):
                    P.emit("tensor", (lambda og16, tt, i, sl: lambda t: t.matmul(
                        psb[sl][:, :],
                        xr1T[:, TOK * i + 128 * tt: TOK * i + 128 * tt + 128],
                        w1q_st[og16 % 2][:, bass.ts(i, 512)],
                        start=(i == 0), stop=(i == 15)))(og16, tt, i, sl),
                        deps=mdeps if i == 0 else (), inc=cT if i == 15 else None)
                k_m1 = cT.n
                P.emit("scalar", (lambda tt, sl: lambda s: s.activation(
                    g1_t[:, :], psb[sl][:, :], AF.Gelu, scale=s1_all[:, tt:tt + 1]))(tt, sl),
                    deps=[(cT.sem, k_m1, cT.name), (cV.sem, k_s1, cV.name)])
                ev10s[sl] = cS.n
                u2dep = [dma[f"u2s{tt % 2}"].dep()] if (og16 * NTO + tt) >= 2 else []
                k_u2 = P.emit("scalar", (lambda tt: lambda s: s.activation(
                    u2_t[tt % 2][:, :], g1_t[:, :], AF.Gelu))(tt), deps=u2dep)
                P.dma("sync", u2_s[bass.ts(tt, 128), bass.ts(og16, 512)],
                      u2_t[tt % 2][:, :], dma[f"u2s{tt % 2}"], deps=[(cS.sem, k_u2, cS.name)])
            u2k[og16] = cT.n
        d_u2_all = [dma["u2s0"].dep(), dma["u2s1"].dep()]

        esD.close()
        esE = contextlib.ExitStack()
        TE2 = lambda name, shape, dt: esE.enter_context(nc.sbuf_tensor(name, shape, dt))
        u2f = [TE2(f"u2f{k}", [128, INNER], BF16) for k in range(2)]
        sqE = TE2("sqE", [128, INNER], F32)
        rtE = TE2("rtE", [128, INNER], F32)
        xr2_t = [TE2(f"xr2_t{k}", [128, INNER], BF16) for k in range(2)]

        ovE = [dma[nm].dep() for nm in ("w1a0", "w1a1", "x1T", "u2s0", "u2s1")] + [
            (cT.sem, cT.n, cT.name), (cS.sem, cS.n, cS.name), (cV.sem, cV.n, cV.name)]
        v3_read = []
        for tt in range(NTO):
            ld = list(ovE) + d_u2_all
            if tt >= 2:
                ld.append((cV.sem, v3_read[tt - 2], cV.name))
            d_u = P.dma("gpsimd", u2f[tt % 2][:, :], u2_s[bass.ts(tt, 128), :],
                        dma[f"u2a{tt % 2}"], deps=ld)
            P.emit("vector", (lambda tt: lambda v: v.reduce_max(st2[:, 8:9], u2f[tt % 2][:, :], axis=AX,
                                                                apply_absolute_value=True))(tt),
                   deps=[d_u])
            k_ss2 = P.emit("scalar", (lambda tt: lambda s: s.activation(
                sqE[:, :], u2f[tt % 2][:, :], AF.Square, accum_out=st2[:, 9:10]))(tt),
                deps=[d_u, (cV.sem, cV.n, cV.name)])
            k_n2 = P.emit("scalar", lambda s: s.activation(st2[:, 10:11], st2[:, 9:10], AF.Sqrt),
                          deps=[(cS.sem, k_ss2, cS.name)])
            P.emit("vector", lambda v: v.tensor_scalar_max(st2[:, 10:11], st2[:, 10:11], 1e-12),
                   deps=[(cS.sem, k_n2, cS.name)])
            P.emit("vector", lambda v: v.reciprocal(st2[:, 11:12], st2[:, 10:11]))
            P.emit("vector", lambda v: v.tensor_mul(st2[:, 8:9], st2[:, 8:9], st2[:, 11:12]))
            P.emit("vector", lambda v: v.tensor_scalar(st2[:, 8:9], st2[:, 8:9], SQI, None, ALU.mult))
            P.emit("vector", lambda v: v.tensor_scalar_max(st2[:, 8:9], st2[:, 8:9], 1e-5))
            P.emit("vector", lambda v: v.tensor_scalar(st2[:, 7:8], st2[:, 8:9], 1.0 / 127.0, None, ALU.mult))
            P.emit("vector", lambda v: v.reciprocal(st2[:, 9:10], st2[:, 8:9]))
            P.emit("vector", lambda v: v.tensor_mul(st2[:, 9:10], st2[:, 9:10], st2[:, 11:12]))
            P.emit("vector", lambda v: v.tensor_scalar(st2[:, 9:10], st2[:, 9:10], 127.0 * SQI, None, ALU.mult))
            P.emit("vector", (lambda tt: lambda v: v.tensor_scalar(
                rtE[:, :], u2f[tt % 2][:, :], st2[:, 9:10], MAGIC, ALU.mult, ALU.add))(tt))
            xdep = [dma[f"x2r{tt % 2}"].dep()] if tt >= 2 else []
            k_x2r = P.emit("vector", (lambda tt: lambda v: v.tensor_scalar(
                xr2_t[tt % 2][:, :], rtE[:, :], -MAGIC, st2[:, 7:8], ALU.add, ALU.mult))(tt),
                deps=xdep)
            v3_read.append(cV.n)
            P.dma("sync", Xr2_s[bass.ts(tt, 128), :], xr2_t[tt % 2][:, :],
                  dma[f"x2r{tt % 2}"], deps=[(cV.sem, k_x2r, cV.name)])
        d_xr2_all = [dma["x2r0"].dep(), dma["x2r1"].dep()]
        if "d_x2" in dbg:
            for tt in range(NTO):
                dd = P.dma("gpsimd", u2f[0][:, 0:2 * D].bitcast(F32),
                           x2_s[bass.ts(tt, 128), :],
                           dma["u2a0"], deps=[(cV.sem, cV.n, cV.name), d_x2_all[0], dma["dbg"].dep()])
                P.dma("sync", dbg["d_x2"][bass.ts(tt, 128), :],
                      u2f[0][:, 0:2 * D].bitcast(F32), dma["dbg"], deps=[dd])
            P.dma("sync", dbg["d_xr1"], xr1_t[0][:, :], dma["dbg"], deps=[(cV.sem, cV.n, cV.name)])
            P.dma("sync", dbg["d_u2"], u2f[1][:, :], dma["dbg"])
            P.dma("sync", dbg["d_xr2"], xr2_t[1][:, :], dma["dbg"])
            P.dma("sync", dbg["d_drs"], drs1_all[:, :], dma["dbg"])
            P.dma("sync", dbg["d_wsc"], wsc_c[:, :], dma["dbg"])
            ddw = P.dma("gpsimd", u2f[0][:, :], w1r_s[1920:2048, :], dma["u2a0"],
                        deps=[dma["dbg"].dep()])
            P.dma("sync", dbg["d_w1r"], u2f[0][:, :], dma["dbg"], deps=[ddw])
            ddw2 = P.dma("gpsimd", u2f[0][:, 0:D], w2r_s[INNER - 128:INNER, :], dma["u2a0"],
                         deps=[dma["dbg"].dep()])
            P.dma("sync", dbg["d_w2r"], u2f[0][:, 0:D], dma["dbg"], deps=[ddw2])

        esE.close()
        esF = contextlib.ExitStack()
        TF = lambda name, shape, dt: esF.enter_context(nc.sbuf_tensor(name, shape, dt))
        xr2T = TF("xr2T", [128, 64 * 1024], BF16)
        w2q_st = [TF(f"w2q_st{k}", [128, 256], BF16) for k in range(2)]
        o2f_t = [TF(f"o2f_t{k}", [128, 512], BF16) for k in range(4)]
        x2l_t = [TF(f"x2l_t{k}", [128, 256], F32) for k in range(2)]
        gtk = TF("gtk", [128, 256], F32)
        o2tok = [TF(f"o2tok{k}", [128, 256], F32) for k in range(2)]
        x3_t = [TF(f"x3_t{k}", [128, 256], F32) for k in range(2)]

        ovF = [dma[nm].dep() for nm in ("u2a0", "u2a1", "x2r0", "x2r1", "dbg")] + [
            (cT.sem, cT.n, cT.name), (cS.sem, cS.n, cS.name), (cV.sem, cV.n, cV.name)]
        w2g = {}
        ev12 = {}
        o2f_free = {}
        tr_free = {}
        k_mm2_half = 0
        for tqh in range(2):
            xdep = list(ovF) + d_xr2_all
            if tqh:
                xdep.append((cT.sem, k_mm2_half, cT.name))
            for i in range(64):
                P.dma("sync", xr2T[:, bass.ts(i, 1024)],
                      Xr2_s[1024 * tqh: 1024 * (tqh + 1), bass.ts(i, 128)],
                      dma["x2T"], deps=xdep, transpose=True)
            d_x2T = dma["x2T"].dep()
            for o2p in range(8):
                for i in range(64):
                    wdep = [d_x2T] + d_wr_all
                    sl = i % 2
                    if sl in w2g:
                        wdep.append((cT.sem, w2g[sl], cT.name))
                    P.dma("gpsimd", w2q_st[sl][:, :],
                          w2r_s[bass.ts(i, 128), 256 * o2p: 256 * (o2p + 1)],
                          dma[f"w2a{sl}"], deps=wdep)
                    d_w2 = dma[f"w2a{sl}"].dep()
                    for o2c in range(2):
                        for tg in range(2):
                            pj = o2c * 2 + tg
                            mdeps = [d_w2]
                            if i == 0 and pj in ev12:
                                mdeps.append((cS.sem, ev12[pj], cS.name))
                            P.emit("tensor", (lambda i, o2c, tg, pj, sl: lambda t: t.matmul(
                                psb[pj][:, :],
                                w2q_st[sl][:, bass.ts(o2c, 128)],
                                xr2T[:, 1024 * i + 512 * tg: 1024 * i + 512 * (tg + 1)],
                                start=(i == 0), stop=(i == 63)))(i, o2c, tg, pj, sl),
                                deps=mdeps,
                                inc=cT if (o2c == 1 and tg == 1) else None)
                    w2g[sl] = cT.n
                k_mm2 = cT.n
                for o2c in range(2):
                    for tg in range(2):
                        pj = o2c * 2 + tg
                        edep = [(cT.sem, k_mm2, cT.name)]
                        if pj in o2f_free:
                            edep.append((cT.sem, o2f_free[pj], cT.name))
                        ev12[pj] = P.emit("scalar", (lambda pj: lambda s: s.activation(
                            o2f_t[pj][:, :], psb[pj][:, :],
                            AF.Identity, bias=0.0, scale=wsc_c[:, 3:4]))(pj),
                            deps=edep)
                for tg in range(2):
                    for tt4 in range(4):
                        tile = 8 * tqh + 4 * tg + tt4
                        par = tt4 % 2
                        for o2c in range(2):
                            pj = o2c * 2 + tg
                            slot = 2 * par + o2c
                            tdep = [(cS.sem, ev12[pj], cS.name), d_cst]
                            if slot in tr_free:
                                tdep.append((cV.sem, tr_free[slot], cV.name))
                            P.emit("tensor", (lambda pj, tt4, par, o2c: lambda t: t.transpose(
                                psb[4 + par][:, :].bitcast(BF16)[:, 128 * o2c: 128 * o2c + 128],
                                o2f_t[pj][:, 128 * tt4: 128 * tt4 + 128],
                                id_sb[:, :]))(pj, tt4, par, o2c), deps=tdep, inc=cT)
                        k_tr2 = cT.n
                        o2f_free[2 * 0 + tg] = k_tr2
                        o2f_free[2 * 1 + tg] = k_tr2
                        d_x2c = P.dma("gpsimd", x2l_t[par][:, :],
                                      x2_s[bass.ts(tile, 128), 256 * o2p: 256 * (o2p + 1)],
                                      dma[f"x2b{par}"],
                                      deps=d_x2_all + [(cV.sem, tr_free.get(2 * par, 0), cV.name)])
                        P.emit("vector", (lambda par: lambda v: v.tensor_copy(
                            o2tok[par][:, :], psb[4 + par][:, :].bitcast(BF16)[:, 0:256]))(par),
                            deps=[(cT.sem, k_tr2, cT.name)])
                        P.emit("vector", (lambda par, o2p: lambda v: v.tensor_mul(
                            gtk[:, :], o2tok[par][:, :],
                            gbF[:, 256 * o2p: 256 * (o2p + 1)]))(par, o2p))
                        x3dep = [d_x2c] + ([dma[f"ob{par}"].dep()] if True else [])
                        k_x3 = P.emit("vector", (lambda par: lambda v: v.tensor_add(
                            x3_t[par][:, :], x2l_t[par][:, :], gtk[:, :]))(par), deps=x3dep)
                        tr_free[2 * par] = k_x3
                        tr_free[2 * par + 1] = k_x3
                        P.dma("sync", out[bass.ts(tile, 128), 256 * o2p: 256 * (o2p + 1)],
                              x3_t[par][:, :], dma[f"ob{par}"], deps=[(cV.sem, k_x3, cV.name)])
            k_mm2_half = cT.n
        fin = [dma["ob0"].dep(), dma["ob1"].dep(), dma["dbg"].dep()]
        P.emit("sync", lambda e: [e.wait_ge(sm_, v_) for sm_, v_, _ in fin][-1])
        esF.close()
        block = E(nc.Block())

        @block.tensor
        def _(t):
            for fn in P.ops["tensor"]:
                fn(t)

        @block.vector
        def _(v):
            for fn in P.ops["vector"]:
                fn(v)

        @block.scalar
        def _(s):
            for fn in P.ops["scalar"]:
                fn(s)

        @block.gpsimd
        def _(g):
            for fn in P.ops["gpsimd"]:
                fn(g)

        @block.sync
        def _(sy):
            for fn in P.ops["sync"]:
                fn(sy)

    return nc, dbg


def kernel(**inputs):
    cores = list(range(8))
    in_maps = [make_core_inputs(i, inputs) for i in cores]
    nc, _ = build()
    res = run_bass_kernel_spmd(nc, in_maps, core_ids=cores)
    outs = [res.results[i]["out"] for i in range(8)]
    full = np.empty((B, L, D), np.float32)
    for i in cores:
        b, s0 = i // 4, (i % 4) * TOK
        full[b, s0:s0 + TOK] = outs[i]
    return full


# version 04 (final)
# speedup vs baseline: 1.0130x; 1.0130x over previous
import contextlib
import math
import os
import numpy as np
import ml_dtypes

import concourse.bass as bass
import concourse.mybir as mybir
from concourse.bass_isa import ReduceOp
from concourse.bass_utils import run_bass_kernel_spmd

F32 = mybir.dt.float32
BF16 = mybir.dt.bfloat16
AF = mybir.ActivationFunctionType
ALU = mybir.AluOpType
AX = mybir.AxisListType.X

B, L, D = 2, 8192, 2048
NH, DH, S = 32, 64, 16
W = 64
TOK = 2048
HALO = 64
TT = TOK + 2 * HALO
NB = TT // W
NT = TT // 128
NTO = TOK // 128
INNER = 4 * D
NHG = 8
HPG = NH // NHG
MAGIC = float(3 * 2.0 ** 22)
RD = 1.0 / D

DBG = []
KPH = int(os.environ.get("KPH", "99"))


def _bf(x):
    return np.ascontiguousarray(x).astype(ml_dtypes.bfloat16)


def make_core_inputs(i, ins):
    b, s0 = i // 4, (i % 4) * TOK
    x = np.asarray(ins["L0_attn_u"])
    xs = np.zeros((TT, D), np.float32)
    lo, hi = s0 - HALO, s0 + TOK + HALO
    clo, chi = max(lo, 0), min(hi, L)
    xs[clo - lo: clo - lo + (chi - clo)] = x[b, clo:chi]

    Wqkv = np.asarray(ins["L0_attn_Wqkv"])
    bqkv = np.asarray(ins["L0_attn_bqkv"])
    perm = []
    for hg in range(NHG):
        for half in range(2):
            for hh in range(HPG):
                h = hg * HPG + hh
                j0 = half * 32
                perm.extend(range(h * DH + j0, h * DH + j0 + 32))
    perm = np.array(perm)
    WqT = np.ascontiguousarray(Wqkv[perm, :].T)
    WkT = np.ascontiguousarray(Wqkv[D + perm, :].T)
    WvT = np.ascontiguousarray(Wqkv[2 * D:, :].T)
    bq = np.ascontiguousarray(bqkv[perm].reshape(16, 128))
    bk = np.ascontiguousarray(bqkv[D + perm].reshape(16, 128))
    bv = np.ascontiguousarray(bqkv[2 * D:].reshape(1, D))

    WoT = np.ascontiguousarray(np.asarray(ins["L0_attn_Wo"]).T)
    bo = np.ascontiguousarray(np.asarray(ins["L0_attn_bo"]).reshape(1, D))

    sinks = np.asarray(ins["L0_attn_sinks"])
    sinksA = np.ascontiguousarray(np.transpose(sinks[:, :, :32], (0, 2, 1)))
    sinksB = np.ascontiguousarray(np.transpose(sinks[:, :, 32:], (0, 2, 1)))
    sinksV = np.ascontiguousarray(sinks)

    w1T = np.ascontiguousarray(np.asarray(ins["L0_ffn_w1"]).T)
    w2T = np.ascontiguousarray(np.asarray(ins["L0_ffn_w2"]).T)
    wmod = np.ascontiguousarray(np.asarray(ins["L0_wmod"]))
    c_row = np.ascontiguousarray(np.asarray(ins["L0_c"])[b].reshape(1, D))

    pos = np.arange(TT, dtype=np.float64) + lo
    inv = 1.0 / (10000.0 ** (np.arange(32, dtype=np.float64) / 32.0))
    ang = pos[None, :] * inv[:, None]
    cos = np.cos(ang); sin = np.sin(ang)
    c128 = np.tile(cos, (4, 1)); s128 = np.tile(sin, (4, 1))
    tabs = dict(ck=_bf(c128), sk=_bf(s128))

    r = np.arange(W)[:, None]
    c = np.arange(3 * W)[None, :]
    m_int = ((c - r >= 0) & (c - r <= 128)).astype(np.float32)
    c2 = np.arange(2 * W)[None, :]
    m_e0 = ((c2 - r >= -64) & (c2 - r <= 64)).astype(np.float32)
    m_all = np.zeros((64, 448), np.float32)
    m_all[:, 0:192] = m_int
    if (s0 - HALO) >= 0:
        m_all[:, 192:320] = m_e0
    if (s0 + TOK + HALO) <= L:
        m_all[:, 320:448] = m_int[:, :128]
    ident = _bf(np.eye(128, dtype=np.float32))

    return dict(
        x_sl=xs, WqT=WqT, WkT=WkT, WvT=WvT, bq=bq, bk=bk, bv=bv,
        WoT=WoT, bo=bo, sinksA=sinksA, sinksB=sinksB, sinksV=sinksV,
        w1T=w1T, w2T=w2T, wmod=wmod, c_row=c_row,
        m_all=_bf(m_all),
        ident=ident, **tabs,
    )


class Chain:
    def __init__(self, sem, name):
        self.sem = sem
        self.name = name
        self.n = 0

    def dep(self):
        return (self.sem, self.n, self.name)


class Prog:
    ENGINES = ("tensor", "vector", "scalar", "gpsimd", "sync")
    AUTO = {"vector", "scalar"}

    def __init__(self, chains):
        self.ops = {e: [] for e in self.ENGINES}
        self.last = {e: {} for e in self.ENGINES}
        self.ch = chains

    def _norm(self, d):
        if isinstance(d, Chain):
            return d.dep()
        if len(d) == 2:
            sem, val = d
            return (sem, val, str(id(sem)))
        return d

    def emit(self, eng, fn, deps=(), inc=None, inc_amt=1):
        lw = self.last[eng]
        deps = [self._norm(d) for d in deps]
        if eng in self.AUTO:
            ch = self.ch[eng]
            deps.append(ch.dep())
            inc, inc_amt = ch, 1
        waits = []
        for sem, val, key in deps:
            if val > lw.get(key, -1):
                lw[key] = val
                waits.append((sem, val))
        if inc is not None:
            inc.n += inc_amt
        inc_sem, amt = (inc.sem if inc else None), inc_amt

        def run(e):
            for sem, val in waits:
                e.wait_ge(sem, val)
            inst = fn(e)
            if inc_sem is not None:
                inst.then_inc(inc_sem, amt)
        self.ops[eng].append(run)
        return inc.n if inc is not None else None

    def dma(self, eng, dst, src, ch, deps=(), **kw):
        self.emit(eng, lambda e: e.dma_start(dst, src, **kw), deps=deps,
                  inc=ch, inc_amt=16)
        return ch.dep()


def build():
    nc = bass.Bass("TRN2", target_bir_lowering=False)
    A = lambda name, shape, dt, kind: nc.dram_tensor(name, shape, dt, kind=kind).ap()
    x_sl = A("x_sl", [TT, D], F32, "ExternalInput")
    WqT_i = A("WqT", [D, D], F32, "ExternalInput")
    WkT_i = A("WkT", [D, D], F32, "ExternalInput")
    WvT_i = A("WvT", [D, D], F32, "ExternalInput")
    bq_i = A("bq", [16, 128], F32, "ExternalInput")
    bk_i = A("bk", [16, 128], F32, "ExternalInput")
    bv_i = A("bv", [1, D], F32, "ExternalInput")
    WoT_i = A("WoT", [D, D], F32, "ExternalInput")
    bo_i = A("bo", [1, D], F32, "ExternalInput")
    sinksA_i = A("sinksA", [NH, 32, S], F32, "ExternalInput")
    sinksB_i = A("sinksB", [NH, 32, S], F32, "ExternalInput")
    sinksV_i = A("sinksV", [NH, S, DH], F32, "ExternalInput")
    w1T_i = A("w1T", [D, INNER], F32, "ExternalInput")
    w2T_i = A("w2T", [INNER, D], F32, "ExternalInput")
    wmod_i = A("wmod", [D, 6 * D], F32, "ExternalInput")
    c_i = A("c_row", [1, D], F32, "ExternalInput")
    ck_i = A("ck", [128, TT], BF16, "ExternalInput")
    sk_i = A("sk", [128, TT], BF16, "ExternalInput")
    m_all_i = A("m_all", [64, 448], BF16, "ExternalInput")
    ident_i = A("ident", [128, 128], BF16, "ExternalInput")

    h_s = A("h_s", [TT, D], BF16, "Internal")
    y_s = A("y_s", [TOK, D], BF16, "Internal")
    x2_s = A("x2_s", [TOK, D], F32, "Internal")
    Xr1_s = A("Xr1_s", [TOK, D], BF16, "Internal")
    w1r_s = A("w1r_s", [D, INNER], BF16, "Internal")
    w2r_s = A("w2r_s", [INNER, D], BF16, "Internal")
    u2_s = A("u2_s", [TOK, INNER], BF16, "Internal")
    Xr2_s = A("Xr2_s", [TOK, INNER], BF16, "Internal")
    rs_s = A("rs_s", [2, 128], F32, "Internal")
    out = A("out", [TOK, D], F32, "ExternalOutput")
    dbg = {}
    if "mod" in DBG:
        dbg["d_mod"] = A("d_mod", [1, 6 * D], F32, "ExternalOutput")
    if "h" in DBG:
        dbg["d_h"] = A("d_h", [TT, D], BF16, "ExternalOutput")
    if "y" in DBG:
        dbg["d_y"] = A("d_y", [TOK, D], BF16, "ExternalOutput")
    if "ffn" in DBG:
        dbg["d_x2"] = A("d_x2", [TOK, D], F32, "ExternalOutput")
        dbg["d_xr1"] = A("d_xr1", [128, D], BF16, "ExternalOutput")
        dbg["d_u2"] = A("d_u2", [128, INNER], BF16, "ExternalOutput")
        dbg["d_xr2"] = A("d_xr2", [128, INNER], BF16, "ExternalOutput")
        dbg["d_w1r"] = A("d_w1r", [128, INNER], BF16, "ExternalOutput")
        dbg["d_w2r"] = A("d_w2r", [128, D], BF16, "ExternalOutput")
        dbg["d_drs"] = A("d_drs", [128, 16], F32, "ExternalOutput")
        dbg["d_wsc"] = A("d_wsc", [128, 4], F32, "ExternalOutput")
    if "att0" in DBG:
        dbg["d_ph"] = A("d_ph", [64, NB * 192], BF16, "ExternalOutput")
        dbg["d_ps"] = A("d_ps", [16, TT], BF16, "ExternalOutput")
        dbg["d_qa"] = A("d_qa", [32, TT], BF16, "ExternalOutput")
        dbg["d_qb"] = A("d_qb", [32, TT], BF16, "ExternalOutput")
        dbg["d_ka"] = A("d_ka", [32, TT], BF16, "ExternalOutput")
        dbg["d_kb"] = A("d_kb", [32, TT], BF16, "ExternalOutput")
        dbg["d_stk"] = A("d_stk", [128, 17 * 256], BF16, "ExternalOutput")
        dbg["d_vb"] = A("d_vb", [128, NT * HPG * 65], BF16, "ExternalOutput")
        dbg["d_yt"] = A("d_yt", [128, NTO * HPG * 64], BF16, "ExternalOutput")

    with contextlib.ExitStack() as es:
        E = es.enter_context
        Tn = lambda name, shape, dt: E(nc.sbuf_tensor(name, shape, dt))

        CH = {e: Chain(E(nc.semaphore(f"ch_{e}")), f"ch_{e}") for e in Prog.ENGINES}
        P = Prog(CH)
        cT, cV, cS = CH["tensor"], CH["vector"], CH["scalar"]
        dma = {nm: Chain(E(nc.semaphore(f"d_{nm}")), f"d_{nm}")
               for nm in ("c", "wm0", "wm1", "x0", "x1", "h0", "h1", "dbg", "out", "misc",
                          "cst", "ws", "snk", "hf0", "hf1", "yt0", "yt1",
                          "yld", "wol", "xa0", "xa1", "x2a0", "x2a1", "xr10", "xr11",
                          "wfa0", "wfa1", "wr0", "wr1", "w1a0", "w1a1", "x1T",
                          "u2s0", "u2s1", "u2a0", "u2a1", "x2r0", "x2r1", "x2T",
                          "w2a0", "w2a1", "x2b0", "x2b1", "ob0", "ob1", "rsa", "rsg")}

        ones1 = Tn("ones1", [1, 128], BF16)
        onecol = Tn("onecol", [128, 1], F32)
        epscol = Tn("epscol", [128, 1], F32)
        saF = Tn("saF", [128, D], BF16)
        ca1F = Tn("ca1F", [128, D], BF16)
        sbF = Tn("sbF", [128, D], BF16)
        cb1F = Tn("cb1F", [128, D], BF16)
        gaF = Tn("gaF", [128, D], BF16)
        gbF = Tn("gbF", [128, D], BF16)
        stm = Tn("stm", [128, 8], F32)
        id_sb = Tn("id_sb", [128, 128], BF16)
        st2 = Tn("st2", [128, 12], F32)
        drs1_all = Tn("drs1_all", [128, 16], F32)
        s1_all = Tn("s1_all", [128, 16], F32)
        wsc_c = Tn("wsc_c", [128, 4], F32)
        rs_c = Tn("rs_c", [128, 12], F32)
        mgc_c = Tn("mgc_c", [128, 1], F32)
        nmg_c = Tn("nmg_c", [128, 1], F32)
        bo_bf = Tn("bo_bf", [1, D], BF16)

        psb = [E(nc.psum_tensor(f"psb{k}", [128, 512], F32)) for k in range(7)]
        ps_tr = E(nc.psum_tensor("ps_tr", [128, 1024], BF16))

        es01 = contextlib.ExitStack()
        T1 = lambda name, shape, dt: es01.enter_context(nc.sbuf_tensor(name, shape, dt))
        modrow = T1("modrow", [1, 6 * D], F32)
        modrow_bf = T1("modrow_bf", [1, 6 * D], BF16)
        gtmp = T1("gtmp", [1, 2 * D], F32)
        gtmp2 = T1("gtmp2", [1, 2 * D], F32)
        c_sil = T1("c_sil", [128, 16], BF16)
        c_silf = T1("c_silf", [128, 16], F32)
        wm_st = [T1(f"wm_st{k}", [128, 4 * 512], BF16) for k in range(2)]
        x_st = [T1(f"x_st{k}", [128, D], F32) for k in range(2)]
        sqscr = T1("sqscr", [128, D], F32)
        xh_t = [T1(f"xh_t{k}", [128, D], BF16) for k in range(2)]
        h_t = [T1(f"h_t{k}", [128, D], BF16) for k in range(2)]

        P.emit("vector", lambda v: v.memset(ones1[:1, :], 1.0))
        P.emit("vector", lambda v: v.memset(onecol[:, :], 1.0))
        P.emit("vector", lambda v: v.memset(epscol[:, :], 1e-5))
        P.emit("vector", lambda v: v.memset(mgc_c[:, :], MAGIC))
        P.emit("vector", lambda v: v.memset(nmg_c[:, :], -MAGIC))
        k_consts = cV.n

        d_c = P.dma("gpsimd", c_silf[:, :], c_i.rearrange("o (f p) -> p (o f)", p=128),
                    dma["c"], allow_slow_non_contiguous=True)
        k_sil = P.emit("scalar", lambda s: s.activation(c_sil[:, :], c_silf[:, :], AF.Silu),
                       deps=[d_c])

        NMOD = 24
        evdone = []
        consumed_at = {}
        for og in range(NMOD):
            for i in range(16):
                n = og * 16 + i
                bch, k = n // 4, n % 4
                if k == 0:
                    ldeps = []
                    if bch >= 2:
                        ldeps.append((cT.sem, consumed_at[bch - 2], cT.name))
                    for kk in range(4):
                        nn = bch * 4 + kk
                        og2, i2 = nn // 16, nn % 16
                        P.dma("gpsimd", wm_st[bch % 2][:, bass.ts(kk, 512)],
                              wmod_i[bass.ts(i2, 128), bass.ts(og2, 512)],
                              dma[f"wm{bch % 2}"], deps=ldeps)
                mdeps = [(dma[f"wm{bch % 2}"].sem, 64 * (bch // 2 + 1), dma[f"wm{bch % 2}"].name)]
                if i == 0:
                    mdeps.append((cS.sem, k_sil, cS.name))
                    if og >= 4:
                        mdeps.append((cS.sem, evdone[og - 4], cS.name))
                need_inc = (k == 3) or (i == 15)
                P.emit("tensor",
                       (lambda og, i, bch, k: lambda t: t.matmul(
                           psb[og % 4][0:1, :], c_sil[:, i:i + 1],
                           wm_st[bch % 2][:, bass.ts(k, 512)],
                           start=(i == 0), stop=(i == 15)))(og, i, bch, k),
                       deps=mdeps, inc=cT if need_inc else None)
                if need_inc:
                    consumed_at[bch] = cT.n
            k_og = cT.n
            evdone.append(P.emit("scalar", lambda s, og=og: s.activation(
                modrow[:1, bass.ts(og, 512)], psb[og % 4][0:1, :], AF.Copy),
                deps=[(cT.sem, k_og, cT.name)]))
        k_modev = evdone[-1]

        P.emit("vector", lambda v: v.tensor_scalar(gtmp[:1, 0:D], modrow[:1, 2 * D:3 * D], -1.0, None, ALU.mult),
               deps=[(cS.sem, k_modev, cS.name)])
        P.emit("vector", lambda v: v.tensor_scalar(gtmp[:1, D:2 * D], modrow[:1, 5 * D:6 * D], -1.0, None, ALU.mult))
        P.emit("vector", lambda v: v.tensor_tensor(gtmp[:1, 0:D], gtmp[:1, 0:D], modrow[:1, 2 * D:3 * D], ALU.max))
        P.emit("vector", lambda v: v.tensor_tensor(gtmp[:1, D:2 * D], gtmp[:1, D:2 * D], modrow[:1, 5 * D:6 * D], ALU.max))
        P.emit("vector", lambda v: v.tensor_scalar_max(gtmp[:1, 0:D], gtmp[:1, 0:D], 0.25))
        P.emit("vector", lambda v: v.tensor_scalar_max(gtmp[:1, D:2 * D], gtmp[:1, D:2 * D], 0.25))
        k_abs = cV.n
        P.emit("scalar", lambda s: s.sign(gtmp2[:1, 0:D], modrow[:1, 2 * D:3 * D]),
               deps=[(cS.sem, k_modev, cS.name)])
        k_sign = P.emit("scalar", lambda s: s.sign(gtmp2[:1, D:2 * D], modrow[:1, 5 * D:6 * D]))
        P.emit("vector", lambda v: v.tensor_mul(modrow[:1, 2 * D:3 * D], gtmp[:1, 0:D], gtmp2[:1, 0:D]),
               deps=[(cS.sem, k_sign, cS.name)])
        P.emit("vector", lambda v: v.tensor_mul(modrow[:1, 5 * D:6 * D], gtmp[:1, D:2 * D], gtmp2[:1, D:2 * D]))
        k_modbf = P.emit("vector", lambda v: v.tensor_copy(modrow_bf[:1, :], modrow[:1, :]))

        BCAST = [(0, saF, True, False), (1, ca1F, True, True), (2, gaF, False, False),
                 (3, sbF, False, False), (4, cb1F, False, True), (5, gbF, False, False)]
        bc_ev = []
        for bi, (vi, dst, _early, plus1) in enumerate(BCAST):
            for j in range(4):
                idx = bi * 4 + j
                mdeps = [(cV.sem, k_modbf, cV.name)]
                if idx >= 4:
                    mdeps.append((cS.sem, bc_ev[idx - 4], cS.name))
                kmm = P.emit("tensor", (lambda vi, j: lambda t: t.matmul(
                    psb[j][:, :], ones1[:1, :],
                    modrow_bf[:1, vi * D + 512 * j: vi * D + 512 * (j + 1)],
                    start=True, stop=True))(vi, j), deps=mdeps, inc=cT)
                if plus1:
                    bc_ev.append(P.emit("scalar", (lambda dst, j: lambda s: s.activation(
                        dst[:, bass.ts(j, 512)], psb[j][:, :],
                        AF.Identity, bias=onecol[:, 0:1], scale=1.0))(dst, j),
                        deps=[(cT.sem, kmm, cT.name), (cV.sem, k_consts, cV.name)]))
                else:
                    bc_ev.append(P.emit("scalar", (lambda dst, j: lambda s: s.activation(
                        dst[:, bass.ts(j, 512)], psb[j][:, :], AF.Copy))(dst, j),
                        deps=[(cT.sem, kmm, cT.name)]))
        k_bcast = cS.n

        v_xread, s_xread, h_dma = [], [], []
        for t in range(NT):
            ldeps = []
            if t >= 2:
                ldeps += [(cV.sem, v_xread[t - 2], cV.name), (cS.sem, s_xread[t - 2], cS.name)]
            d_x = P.dma("gpsimd", x_st[t % 2][:, :], x_sl[bass.ts(t, 128), :],
                        dma[f"x{t % 2}"], deps=ldeps)
            P.emit("vector", lambda v, t=t: v.reduce_sum(stm[:, 0:1], x_st[t % 2][:, :], axis=AX),
                   deps=[d_x])
            k_mean = P.emit("vector", lambda v: v.tensor_scalar(stm[:, 2:3], stm[:, 0:1], RD, None, ALU.mult))
            k_sq = P.emit("scalar", lambda s, t=t: s.activation(
                sqscr[:, :], x_st[t % 2][:, :], AF.Square, accum_out=stm[:, 1:2]),
                deps=[d_x, (cV.sem, k_mean, cV.name)])
            P.emit("vector", lambda v: v.tensor_scalar(stm[:, 7:8], stm[:, 1:2], RD, None, ALU.mult),
                   deps=[(cS.sem, k_sq, cS.name)])
            P.emit("vector", lambda v: v.tensor_mul(stm[:, 3:4], stm[:, 2:3], stm[:, 2:3]))
            k_var = P.emit("vector", lambda v: v.tensor_sub(stm[:, 3:4], stm[:, 7:8], stm[:, 3:4]))
            k_std = P.emit("scalar", lambda s: s.activation(stm[:, 4:5], stm[:, 3:4], AF.Sqrt, bias=epscol[:, 0:1]),
                           deps=[(cV.sem, k_var, cV.name)])
            P.emit("vector", lambda v: v.reciprocal(stm[:, 5:6], stm[:, 4:5]),
                   deps=[(cS.sem, k_std, cS.name)])
            P.emit("vector", lambda v: v.tensor_mul(stm[:, 7:8], stm[:, 2:3], stm[:, 5:6]))
            k_inv = P.emit("vector", lambda v: v.tensor_scalar(stm[:, 6:7], stm[:, 7:8], -1.0, None, ALU.mult))
            xdeps = [(cV.sem, k_inv, cV.name)]
            if t >= 2:
                xdeps.append((cV.sem, v_xread[t - 2], cV.name))
            k_xh = P.emit("scalar", lambda s, t=t: s.activation(
                xh_t[t % 2][:, :], x_st[t % 2][:, :], AF.Identity,
                bias=stm[:, 6:7], scale=stm[:, 5:6]), deps=xdeps)
            s_xread.append(k_xh)
            hdeps = [(cS.sem, k_xh, cS.name), (cS.sem, k_bcast, cS.name)]
            if t >= 2:
                hdeps.append(h_dma[t - 2])
            P.emit("vector", lambda v, t=t: v.tensor_mul(h_t[t % 2][:, :], xh_t[t % 2][:, :], ca1F[:, :]),
                   deps=hdeps)
            v_xread.append(cV.n)
            k_h = P.emit("vector", lambda v, t=t: v.tensor_add(h_t[t % 2][:, :], h_t[t % 2][:, :], saF[:, :]))
            d_h = P.dma("sync", h_s[bass.ts(t, 128), :], h_t[t % 2][:, :], dma[f"h{t % 2}"],
                        deps=[(cV.sem, k_h, cV.name)])
            if "d_h" in dbg:
                d_h = P.dma("sync", dbg["d_h"][bass.ts(t, 128), :], h_t[t % 2][:, :], dma[f"h{t % 2}"])
            h_dma.append(d_h)

        if "d_mod" in dbg:
            P.dma("sync", dbg["d_mod"], modrow[:1, :], dma["dbg"],
                  deps=[(cV.sem, k_modbf, cV.name)])

        es01.close()

        esA = contextlib.ExitStack()
        TA = lambda name, shape, dt: esA.enter_context(nc.sbuf_tensor(name, shape, dt))
        hfm_st = [TA("hfm0", [128, 16 * 512], BF16), TA("hfm1", [128, 16 * 512], BF16)]
        wqa_st = TA("wqa_st", [128, 2048], BF16)
        wqb_st = TA("wqb_st", [128, 2048], BF16)
        wka_st = TA("wka_st", [128, 2048], BF16)
        wkb_st = TA("wkb_st", [128, 2048], BF16)
        wv_st = TA("wv_st", [128, 4096], BF16)
        w_st = [wqa_st, wqb_st, wka_st, wkb_st]
        preA = TA("preA", [128, TT], BF16)
        preB = TA("preB", [128, TT], BF16)
        preKA = TA("preKA", [128, TT], BF16)
        preKB = TA("preKB", [128, TT], BF16)
        rt1q = TA("rt1q", [128, TT], BF16)
        rt2q = TA("rt2q", [128, TT], BF16)
        rt1k = TA("rt1k", [128, TT], BF16)
        rt2k = TA("rt2k", [128, TT], BF16)
        ck_sb = TA("ck_sb", [128, TT], BF16)
        sk_sb = TA("sk_sb", [128, TT], BF16)
        qa_h = [TA(f"qa_h{k}", [32, TT], BF16) for k in range(2)]
        qb_h = [TA(f"qb_h{k}", [32, TT], BF16) for k in range(2)]
        ka_h = [TA(f"ka_h{k}", [32, TT], BF16) for k in range(2)]
        kb_h = [TA(f"kb_h{k}", [32, TT], BF16) for k in range(2)]
        v_buf = TA("v_buf", [128, NT * HPG * 65], BF16)
        sA_sb = TA("sA_sb", [32, HPG * S], BF16)
        sB_sb = TA("sB_sb", [32, HPG * S], BF16)
        sV_sb = TA("sV_sb", [16, HPG * 65], BF16)
        p_h = TA("p_h", [64, NB * 192], BF16)
        psink = TA("psink", [16, TT], BF16)
        stk = TA("stk", [128, 17 * 256], BF16)
        rcp = TA("rcp", [128, 1], F32)
        y_tok = [TA(f"y_tok{k}", [128, 16 * 64], BF16) for k in range(2)]
        m_all_sb = TA("m_all_sb", [64, 448], BF16)
        bq_sb = TA("bq_sb", [128, 16], F32)
        bk_sb = TA("bk_sb", [128, 16], F32)
        bv_bf = TA("bv_bf", [1, D], BF16)

        ov_deps = [dma[nm].dep() for nm in ("c", "wm0", "wm1", "x0", "x1", "h0", "h1", "dbg")] + [
            (cS.sem, cS.n, cS.name), (cV.sem, cV.n, cV.name), (cT.sem, cT.n, cT.name)]
        P.dma("gpsimd", ck_sb[:, :], ck_i, dma["cst"], deps=ov_deps)
        P.dma("gpsimd", sk_sb[:, :], sk_i, dma["cst"])
        P.dma("gpsimd", m_all_sb[:, :], m_all_i, dma["cst"])
        P.dma("gpsimd", id_sb[:, :], ident_i, dma["cst"])
        P.dma("gpsimd", bq_sb[:, :], bq_i.rearrange("c p -> p c"), dma["cst"],
              allow_slow_non_contiguous=True)
        P.dma("gpsimd", bk_sb[:, :], bk_i.rearrange("c p -> p c"), dma["cst"],
              allow_slow_non_contiguous=True)
        d_cst = P.dma("gpsimd", bv_bf[:1, :], bv_i, dma["cst"])
        P.emit("vector", lambda v: v.memset(
            bass.AP(v_buf, 64, [[NT * HPG * 65, 128], [65, NT * HPG], [1, 1]]), 1.0),
            deps=ov_deps)
        P.emit("vector", lambda v: v.memset(
            bass.AP(sV_sb, 64, [[HPG * 65, 16], [65, HPG], [1, 1]]), 1.0))
        P.emit("vector", lambda v: v.memset(stk[:, :], 0.0))
        k_init = P.emit("vector", lambda v: v.memset(p_h[0:64, :], 0.0))

        TGS = [(0, 512), (512, 512), (1024, 512), (1536, 512), (2048, 128)]
        ev_qk = {}
        ev_v = {}
        hfm_free = [0, 0]
        wset_used = 0
        sslot_free = [0, 0, 0, 0]
        sink_ev_last = 0
        psink_free = 0
        ph_free = 0
        stk_free = 0
        rcp_free = 0
        pv_slot_free = [0, 0]
        ytok_free = 0
        yt_dma_prev = None
        dbg_pend = None
        last_att = 0

        for hg in range(NHG):
            wdeps = ([(cT.sem, wset_used, cT.name)] if hg else []) + (
                [dbg_pend] if dbg_pend else [])
            for i in range(16):
                P.dma("gpsimd", wqa_st[:, bass.ts(i, 128)],
                      WqT_i[bass.ts(i, 128), 256 * hg: 256 * hg + 128], dma["ws"], deps=wdeps)
                P.dma("gpsimd", wqb_st[:, bass.ts(i, 128)],
                      WqT_i[bass.ts(i, 128), 256 * hg + 128: 256 * hg + 256], dma["ws"], deps=wdeps)
                P.dma("gpsimd", wka_st[:, bass.ts(i, 128)],
                      WkT_i[bass.ts(i, 128), 256 * hg: 256 * hg + 128], dma["ws"], deps=wdeps)
                P.dma("gpsimd", wkb_st[:, bass.ts(i, 128)],
                      WkT_i[bass.ts(i, 128), 256 * hg + 128: 256 * hg + 256], dma["ws"], deps=wdeps)
                P.dma("gpsimd", wv_st[:, bass.ts(i, 256)],
                      WvT_i[bass.ts(i, 128), 256 * hg: 256 * hg + 256], dma["ws"], deps=wdeps)
            d_ws = dma["ws"].dep()
            sdeps = [(cT.sem, last_att, cT.name)] if hg else []
            for hh in range(HPG):
                h = hg * HPG + hh
                P.dma("gpsimd", sA_sb[0:32, bass.ts(hh, S)], sinksA_i[h], dma["snk"], deps=sdeps)
                P.dma("gpsimd", sB_sb[0:32, bass.ts(hh, S)], sinksB_i[h], dma["snk"], deps=sdeps)
                P.dma("gpsimd", sV_sb[0:16, 65 * hh: 65 * hh + 64], sinksV_i[h], dma["snk"], deps=sdeps)
            d_snk = dma["snk"].dep()

            for tg, (tgo, tgw) in enumerate(TGS):
                hsl = (hg * 5 + tg) % 2
                lastt = (tgo + tgw - 1) // 128
                hdeps = [h_dma[lastt], (cT.sem, hfm_free[hsl], cT.name)] + (
                    ov_deps if (hg, tg) == (0, 0) else [])
                for i in range(16):
                    P.dma("sync", hfm_st[hsl][:, 512 * i: 512 * i + tgw],
                          h_s[tgo:tgo + tgw, bass.ts(i, 128)], dma[f"hf{hsl}"],
                          deps=hdeps, transpose=True)
                d_hf = dma[f"hf{hsl}"].dep()
                for oc in range(4):
                    mdeps = [d_hf, d_ws]
                    prev = ev_qk.get((tg - 1 if tg else 4, oc))
                    if prev:
                        mdeps.append((cS.sem, prev, cS.name))
                    for i in range(16):
                        P.emit("tensor", (lambda oc, i, hsl, tgw: lambda t: t.matmul(
                            psb[oc][:, 0:tgw],
                            w_st[oc][:, bass.ts(i, 128)],
                            hfm_st[hsl][:, 512 * i: 512 * i + tgw],
                            start=(i == 0), stop=(i == 15)))(oc, i, hsl, tgw),
                            deps=mdeps if i == 0 else (), inc=cT if i == 15 else None)
                    k_mm = cT.n
                    pre_t = [preA, preB, preKA, preKB][oc]
                    bias = [bq_sb[:, 2 * hg:2 * hg + 1], bq_sb[:, 2 * hg + 1:2 * hg + 2],
                            bk_sb[:, 2 * hg:2 * hg + 1], bk_sb[:, 2 * hg + 1:2 * hg + 2]][oc]
                    ev_qk[(tg, oc)] = P.emit("scalar", (lambda pre_t, bias, tgo, tgw, oc: lambda s: s.activation(
                        pre_t[:, tgo:tgo + tgw], psb[oc][:, 0:tgw],
                        AF.Identity, bias=bias, scale=1.0))(pre_t, bias, tgo, tgw, oc),
                        deps=[(cT.sem, k_mm, cT.name), d_cst])
                for tt in range(tgw // 128):
                    t_glob = tgo // 128 + tt
                    sl = t_glob % 3
                    vdeps = [d_hf, d_ws, d_cst, (cV.sem, k_init, cV.name)] + (
                        [dbg_pend] if dbg_pend else [])
                    if ev_v.get(sl):
                        vdeps.append((cS.sem, ev_v[sl], cS.name))
                    P.emit("tensor", (lambda sl, hg: lambda t: t.matmul(
                        psb[4 + sl][:, 0:256],
                        ones1[:1, :], bv_bf[:1, 256 * hg: 256 * hg + 256],
                        start=True, stop=False))(sl, hg), deps=vdeps)
                    for i in range(16):
                        P.emit("tensor", (lambda sl, i, hsl, tt: lambda t: t.matmul(
                            psb[4 + sl][:, 0:256],
                            hfm_st[hsl][:, 512 * i + 128 * tt: 512 * i + 128 * tt + 128],
                            wv_st[:, bass.ts(i, 256)],
                            start=False, stop=(i == 15)))(sl, i, hsl, tt),
                            inc=cT if i == 15 else None)
                    k_vmm = cT.n
                    ev_v[sl] = P.emit("scalar", (lambda t_glob, sl: lambda s: s.activation(
                        v_buf[:, 260 * t_glob: 260 * (t_glob + 1)].rearrange(
                            "p (h c) -> p h c", c=65)[:, :, 0:64],
                        psb[4 + sl][:, 0:256].rearrange("p (h c) -> p h c", c=64),
                        AF.Copy))(t_glob, sl),
                        deps=[(cT.sem, k_vmm, cT.name)])
                hfm_free[hsl] = cT.n
            wset_used = cT.n
            k_qkev = max(ev_qk[(4, oc)] for oc in range(4))
            k_vev = max(ev_v.values())

            rdeps = [(cS.sem, k_qkev, cS.name), d_cst, (cT.sem, last_att, cT.name)]
            P.emit("vector", lambda v: v.tensor_mul(rt1q[:, :], preA[:, :], sk_sb[:, :]), deps=rdeps)
            P.emit("vector", lambda v: v.tensor_mul(rt2q[:, :], preB[:, :], sk_sb[:, :]))
            P.emit("vector", lambda v: v.tensor_mul(preA[:, :], preA[:, :], ck_sb[:, :]))
            P.emit("vector", lambda v: v.tensor_mul(preB[:, :], preB[:, :], ck_sb[:, :]))
            P.emit("vector", lambda v: v.tensor_mul(rt1k[:, :], preKA[:, :], sk_sb[:, :]))
            P.emit("vector", lambda v: v.tensor_mul(rt2k[:, :], preKB[:, :], sk_sb[:, :]))
            P.emit("vector", lambda v: v.tensor_mul(preKA[:, :], preKA[:, :], ck_sb[:, :]))
            P.emit("vector", lambda v: v.tensor_mul(preKB[:, :], preKB[:, :], ck_sb[:, :]))
            k_ropep = cV.n

            qkh_free = [0, 0]
            for hh in range(HPG):
                sl2 = hh % 2
                fdeps = [(cV.sem, k_ropep, cV.name), (cT.sem, qkh_free[sl2], cT.name)]
                P.emit("vector", (lambda hh, sl2: lambda v: v.tensor_sub(
                    qa_h[sl2][0:32, :], preA[32 * hh:32 * hh + 32, :], rt2q[32 * hh:32 * hh + 32, :]))(hh, sl2),
                    deps=fdeps)
                P.emit("vector", (lambda hh, sl2: lambda v: v.tensor_add(
                    qb_h[sl2][0:32, :], rt1q[32 * hh:32 * hh + 32, :], preB[32 * hh:32 * hh + 32, :]))(hh, sl2))
                P.emit("vector", (lambda hh, sl2: lambda v: v.tensor_sub(
                    ka_h[sl2][0:32, :], preKA[32 * hh:32 * hh + 32, :], rt2k[32 * hh:32 * hh + 32, :]))(hh, sl2))
                k_rope = P.emit("vector", (lambda hh, sl2: lambda v: v.tensor_add(
                    kb_h[sl2][0:32, :], rt1k[32 * hh:32 * hh + 32, :], preKB[32 * hh:32 * hh + 32, :]))(hh, sl2))
                for qg, (qgo, qgw) in enumerate(TGS):
                    sdeps2 = [(cV.sem, k_rope, cV.name), d_snk,
                              (cS.sem, sink_ev_last, cS.name),
                              (cT.sem, psink_free, cT.name)]
                    P.emit("tensor", (lambda hh, qgo, qgw: lambda t: t.matmul(
                        psb[4][0:16, 0:qgw],
                        sA_sb[0:32, bass.ts(hh, S)], qa_h[hh % 2][0:32, qgo:qgo + qgw],
                        start=True, stop=False))(hh, qgo, qgw), deps=sdeps2)
                    P.emit("tensor", (lambda hh, qgo, qgw: lambda t: t.matmul(
                        psb[4][0:16, 0:qgw],
                        sB_sb[0:32, bass.ts(hh, S)], qb_h[hh % 2][0:32, qgo:qgo + qgw],
                        start=False, stop=True))(hh, qgo, qgw), inc=cT)
                    k_smm = cT.n
                    sink_ev_last = P.emit("scalar", (lambda qgo, qgw: lambda s: s.activation(
                        psink[0:16, qgo:qgo + qgw], psb[4][0:16, 0:qgw],
                        AF.Exp, scale=0.125))(qgo, qgw), deps=[(cT.sem, k_smm, cT.name)])
                exp_local = {}
                for m in range(NB):
                    so = 64 * max(m - 1, 0)
                    spw = 64 * (min(m + 1, NB - 1) - max(m - 1, 0) + 1)

                    bdeps = [(cV.sem, k_rope, cV.name),
                             (cS.sem, sslot_free[m % 4], cS.name),
                             (cV.sem, ph_free, cV.name)]
                    P.emit("tensor", (lambda hh, m, so, spw: lambda t: t.matmul(
                        psb[m % 4][0:64, 0:spw],
                        ka_h[hh % 2][0:32, 64 * m:64 * m + 64], qa_h[hh % 2][0:32, so:so + spw],
                        start=True, stop=False))(hh, m, so, spw), deps=bdeps)
                    P.emit("tensor", (lambda hh, m, so, spw: lambda t: t.matmul(
                        psb[m % 4][0:64, 0:spw],
                        kb_h[hh % 2][0:32, 64 * m:64 * m + 64], qb_h[hh % 2][0:32, so:so + spw],
                        start=False, stop=True))(hh, m, so, spw), inc=cT)
                    k_bmm = cT.n
                    sslot_free[m % 4] = exp_local[m] = P.emit(
                        "scalar", (lambda m, spw: lambda s: s.activation(
                            p_h[0:64, 192 * m: 192 * m + spw], psb[m % 4][0:64, 0:spw],
                            AF.Exp, scale=0.125))(m, spw),
                        deps=[(cT.sem, k_bmm, cT.name), (cV.sem, ph_free, cV.name)])
                k_exp = cS.n
                stk_deps = [(cS.sem, k_exp, cS.name), (cT.sem, stk_free, cT.name), d_cst]
                P.emit("vector", lambda v: v.tensor_mul(
                    stk[0:64, 64:192], p_h[0:64, 0:128], m_all_sb[0:64, 192:320]),
                    deps=stk_deps)
                P.emit("vector", lambda v: v.tensor_tensor(
                    stk[0:64, 256:256 + 16 * 256].rearrange("p (j c) -> p j c", c=256)[:, :, 0:192],
                    p_h[0:64, 384:384 + 16 * 384].rearrange("p (u c) -> p u c", c=384)[:, :, 0:192],
                    bass.AP(m_all_sb, 0, [[448, 64], [0, 16], [1, 192]]),
                    ALU.mult))
                P.emit("vector", lambda v: v.tensor_tensor(
                    stk[64:128, 64:64 + 16 * 256].rearrange("p (j c) -> p j c", c=256)[:, :, 0:192],
                    p_h[0:64, 192:192 + 16 * 384].rearrange("p (u c) -> p u c", c=384)[:, :, 0:192],
                    bass.AP(m_all_sb, 0, [[448, 64], [0, 16], [1, 192]]),
                    ALU.mult))
                k_stk = P.emit("vector", lambda v: v.tensor_mul(
                    stk[64:128, 256 * 16 + 64: 256 * 16 + 192], p_h[0:64, 192 * 33:192 * 33 + 128],
                    m_all_sb[0:64, 320:448]))
                ph_free = k_stk
                for i in range(1, 17):
                    pst = psb[5 + (i % 2)]
                    vdeps2 = [(cV.sem, k_stk, cV.name), (cS.sem, sink_ev_last, cS.name),
                              (cS.sem, k_vev, cS.name),
                              (cS.sem, pv_slot_free[i % 2], cS.name)]
                    P.emit("tensor", (lambda i, hh, pst: lambda t: t.matmul(
                        pst[:, 0:65],
                        stk[:, 256 * (i - 1) + 128: 256 * (i - 1) + 256],
                        v_buf[:, 260 * (i - 1) + 65 * hh: 260 * (i - 1) + 65 * hh + 65],
                        start=True, stop=False))(i, hh, pst), deps=vdeps2)
                    P.emit("tensor", (lambda i, hh, pst: lambda t: t.matmul(
                        pst[:, 0:65],
                        stk[:, 256 * i: 256 * i + 128],
                        v_buf[:, 260 * i + 65 * hh: 260 * i + 65 * hh + 65],
                        start=False, stop=False))(i, hh, pst))
                    P.emit("tensor", (lambda i, hh, pst: lambda t: t.matmul(
                        pst[:, 0:65],
                        psink[0:16, 64 * (2 * i - 1): 64 * (2 * i - 1) + 128],
                        sV_sb[0:16, 65 * hh: 65 * hh + 65],
                        start=False, stop=True))(i, hh, pst), inc=cT)
                    k_pv = cT.n
                    P.emit("vector", (lambda pst: lambda v: v.reciprocal(
                        rcp[:, 0:1], pst[:, 64:65]))(pst),
                        deps=[(cT.sem, k_pv, cT.name), (cS.sem, rcp_free, cS.name)])
                    k_rcp = cV.n
                    ytdep = [(cV.sem, k_rcp, cV.name)] + (
                        [dma[f"yt{hh % 2}"].dep()] if i == 1 else [])
                    rcp_free = pv_slot_free[i % 2] = P.emit(
                        "scalar", (lambda i, hh, pst: lambda s: s.activation(
                            y_tok[hh % 2][:, 64 * (i - 1): 64 * i],
                            pst[:, 0:64], AF.Identity,
                            bias=0.0, scale=rcp[:, 0:1]))(i, hh, pst),
                        deps=ytdep)
                stk_free = psink_free = cT.n
                qkh_free[sl2] = cT.n
                yt_dma_prev = P.dma(
                    "sync",
                    y_s.rearrange("(a p) c -> p a c", p=128)[
                        :, :, 256 * hg + 64 * hh: 256 * hg + 64 * hh + 64],
                    y_tok[hh % 2][:, :].rearrange("p (a c) -> p a c", c=64),
                    dma[f"yt{hh % 2}"], deps=[(cS.sem, cS.n, cS.name)])
                if "d_ph" in dbg and hg == 0 and hh == 0:
                    P.dma("sync", dbg["d_ph"], p_h[0:64, :], dma["dbg"],
                          deps=[(cV.sem, k_stk, cV.name)])
                    P.dma("sync", dbg["d_ps"], psink[0:16, :], dma["dbg"],
                          deps=[(cS.sem, sink_ev_last, cS.name)])
                    P.dma("sync", dbg["d_qa"], qa_h[0][0:32, :], dma["dbg"])
                    P.dma("sync", dbg["d_qb"], qb_h[0][0:32, :], dma["dbg"])
                    P.dma("sync", dbg["d_ka"], ka_h[0][0:32, :], dma["dbg"])
                    P.dma("sync", dbg["d_kb"], kb_h[0][0:32, :], dma["dbg"])
                    P.dma("sync", dbg["d_stk"], stk[:, :], dma["dbg"])
                    P.dma("sync", dbg["d_vb"], v_buf[:, :], dma["dbg"],
                          deps=[(cS.sem, k_vev, cS.name)])
                    P.emit("vector", lambda v: v.memset(stm[:, 7:8], 0.0),
                           deps=[dma["dbg"].dep()])
                    dbg_pend = dma["dbg"].dep()
            last_att = cT.n
            if "d_yt" in dbg and hg == 0:
                P.dma("sync", dbg["d_yt"][:, 0:1024], y_tok[1][:, :], dma["dbg"],
                      deps=[(cS.sem, cS.n, cS.name)])
                dbg_pend = dma["dbg"].dep()
        k_att_T = cT.n
        d_yt_all = dma["yt0"].dep()
        d_yt_all2 = dma["yt1"].dep()

        def emit_stub_out(scope_T, deps):
            stub = scope_T("stub_st", [128, D], F32)
            for t in range(NTO):
                d_m = P.dma("sync", stub[:, :], x_sl[128 * t + HALO:128 * (t + 1) + HALO, :],
                            dma["x2b0"], deps=deps + [dma["ob0"].dep()])
                P.dma("sync", out[bass.ts(t, 128), :], stub[:, :], dma["ob0"], deps=[d_m])
            fin0 = [dma["ob0"].dep(), dma["dbg"].dep()]
            P.emit("sync", lambda e: [e.wait_ge(sm_, v_) for sm_, v_, _ in fin0][-1])

        esA.close()
        esB = contextlib.ExitStack()
        TB = lambda name, shape, dt: esB.enter_context(nc.sbuf_tensor(name, shape, dt))
        if KPH < 1:
            emit_stub_out(TB, [d_yt_all, (cT.sem, cT.n, cT.name), (cS.sem, cS.n, cS.name),
                               (cV.sem, cV.n, cV.name)])
        yT_full = TB("yT_full", [128, 16 * TOK], BF16)
        wo_full = TB("wo_full", [128, 16 * D], BF16)
        _xs2 = TB("x_st20", [128, D], F32)
        x_st2 = [_xs2, _xs2]
        scr8 = TB("scr8", [128, D], F32)
        attn_t = scr8
        sq2 = scr8
        rtmp = scr8
        _x2 = TB("x2_t0", [128, D], F32)
        x2_t = [_x2, _x2]
        xh2_t = TB("xh2_t", [128, D], BF16)
        _h2 = TB("h2_t0", [128, D], BF16)
        h2_t = [_h2, _h2]
        _xr1 = TB("xr1_t0", [128, D], BF16)
        xr1_t = [_xr1, _xr1]
        SQD = float(math.sqrt(D))
        SQI = float(math.sqrt(INNER))

        CQ = 1024
        wqB_f32 = [TB(f"wqB_f32{k}", [128, CQ], F32) for k in range(2)]
        wqB_out = [TB(f"wqB_out{k}", [128, CQ], BF16) for k in range(2)]
        ovB = [dma[nm].dep() for nm in ("cst", "ws", "snk", "hf0", "hf1", "yt0", "yt1", "dbg")] + [
            (cT.sem, cT.n, cT.name), (cS.sem, cS.n, cS.name), (cV.sem, cV.n, cV.name)]
        P.dma("gpsimd", bo_bf[:1, :], bo_i, dma["wol"], deps=ovB)
        for i in range(16):
            P.dma("sync", yT_full[:, bass.ts(i, TOK)], y_s[:, bass.ts(i, 128)],
                  dma["yld"], deps=ovB + [d_yt_all, d_yt_all2], transpose=True)
            P.dma("gpsimd", wo_full[:, bass.ts(i, D)], WoT_i[bass.ts(i, 128), :],
                  dma["wol"], deps=ovB)
        d_yld = dma["yld"].dep()
        d_wol = dma["wol"].dep()

        def quant_weight_gen(bufs, w_in, w_rs, nrt, nct, col, start_deps):
            wq_f32, wq_out = bufs
            wq_abs = wq_out[0]
            CW = CQ
            nchunk = nrt * nct
            absk, addk, s1k = [], [], []
            rsa0 = dma["rsa"].dep()
            rsg0 = dma["rsg"].dep()
            for ci in range(nchunk):
                rt_, ct_ = ci // nct, ci % nct
                ld = list(start_deps) if ci < 2 else [(cS.sem, absk[ci - 2], cS.name)]
                d_w = P.dma("gpsimd", wq_f32[ci % 2][:, :],
                            w_in[bass.ts(rt_, 128), bass.ts(ct_, CW)],
                            dma[f"wfa{ci % 2}"], deps=ld)
                adep = [d_w] + ([rsa0, rsg0] if ci == 0 else [])
                if ci >= 8:
                    adep.append((cV.sem, addk[ci - 8], cV.name))
                absk.append(P.emit("scalar", (lambda ci: lambda s: s.activation(
                    wq_abs[:, :], wq_f32[ci % 2][:, :], AF.Abs,
                    accum_out=rs_c[:, 2 + ci % 8: 3 + ci % 8]))(ci), deps=adep))
                if ci == 0:
                    addk.append(P.emit("vector", lambda v: v.tensor_copy(rs_c[:, 0:1], rs_c[:, 2:3]),
                                       deps=[(cS.sem, absk[-1], cS.name), rsa0]))
                else:
                    addk.append(P.emit("vector", (lambda ci: lambda v: v.tensor_add(
                        rs_c[:, 0:1], rs_c[:, 0:1], rs_c[:, 2 + ci % 8: 3 + ci % 8]))(ci),
                        deps=[(cS.sem, absk[-1], cS.name)]))
                yield
            d1 = P.dma("sync", rs_s[0:1, :], rs_c[:, 0:1],
                       dma["rsa"], deps=[(cV.sem, addk[-1], cV.name), rsg0],
                       allow_slow_non_contiguous=True)
            d2 = P.dma("gpsimd", wq_abs[:, 0:256].bitcast(F32)[0:1, 0:128],
                       rs_s[0:1, :], dma["rsg"], deps=[d1, (cS.sem, absk[-1], cS.name)])
            k_tot = P.emit("vector", lambda v: v.reduce_sum(
                rs_c[0:1, 1:2], wq_abs[:, 0:256].bitcast(F32)[0:1, 0:128], axis=AX), deps=[d2])
            d3 = P.dma("sync", rs_s[1:2, 0:1], rs_c[0:1, 1:2], dma["rsa"],
                       deps=[(cV.sem, k_tot, cV.name)])
            d4 = P.dma("gpsimd", rs_c[:, 1:2], rs_s[1:2, 0:1].to_broadcast([128, 1]),
                       dma["rsg"], deps=[d3])
            NW = 128.0 * nchunk * CW
            P.emit("vector", (lambda col, NW: lambda v: v.tensor_scalar(
                wsc_c[:, col + 1:col + 2], rs_c[:, 1:2], 1.0 / NW, None, ALU.mult))(col, NW),
                deps=[d4])
            P.emit("vector", (lambda col: lambda v: v.tensor_scalar_max(
                wsc_c[:, col + 1:col + 2], wsc_c[:, col + 1:col + 2], 1e-5))(col))
            k_wm = P.emit("vector", (lambda col: lambda v: v.reciprocal(
                wsc_c[:, col:col + 1], wsc_c[:, col + 1:col + 2]))(col))
            yield
            for ci in range(nchunk):
                rt_, ct_ = ci // nct, ci % nct
                ld = [(cV.sem, k_wm, cV.name)]
                if ci >= 2:
                    ld.append((cS.sem, s1k[ci - 2], cS.name))
                d_w = P.dma("gpsimd", wq_f32[ci % 2][:, :],
                            w_in[bass.ts(rt_, 128), bass.ts(ct_, CW)],
                            dma[f"wfa{ci % 2}"], deps=ld)
                P.emit("scalar", (lambda ci, col: lambda s: s.activation(
                    wq_f32[ci % 2][:, :], wq_f32[ci % 2][:, :], AF.Identity,
                    bias=mgc_c[:, 0:1], scale=wsc_c[:, col:col + 1]))(ci, col),
                    deps=[d_w, (cV.sem, k_consts, cV.name)])
                P.emit("scalar", (lambda ci: lambda s: s.activation(
                    wq_f32[ci % 2][:, :], wq_f32[ci % 2][:, :], AF.Identity,
                    bias=nmg_c[:, 0:1], scale=1.0))(ci))
                k_sgn = P.emit("scalar", (lambda ci: lambda s: s.sign(
                    wq_out[ci % 2][:, :], wq_f32[ci % 2][:, :]))(ci),
                    deps=[dma[f"wr{ci % 2}"].dep()] + (
                        [(cV.sem, k_wm, cV.name)] if ci < 2 else []))
                s1k.append(k_sgn)
                P.dma("sync", w_rs[bass.ts(rt_, 128), bass.ts(ct_, CW)], wq_out[ci % 2][:, :],
                      dma[f"wr{ci % 2}"], deps=[(cS.sem, k_sgn, cS.name)])
                yield

        qg1 = quant_weight_gen((wqB_f32, wqB_out),
                               w1T_i, w1r_s, 16, 8, 0, ovB)

        ev8 = {}
        v2_read = []
        h2_free = []
        x2cons = []
        for tt in range(NTO):
            xdeps = list(ovB) if tt < 1 else [(cV.sem, v2_read[tt - 1], cV.name)]
            d_x = P.dma("gpsimd", x_st2[tt % 2][:, :],
                        x_sl[128 * tt + HALO: 128 * (tt + 1) + HALO, :],
                        dma[f"xa{tt % 2}"], deps=xdeps)
            for oc in range(4):
                pdeps = [d_yld, d_wol, d_cst]
                if oc in ev8:
                    pdeps.append((cV.sem, ev8[oc], cV.name))
                P.emit("tensor", (lambda oc: lambda t: t.matmul(
                    psb[oc][:, :], ones1[:1, :], bo_bf[:1, bass.ts(oc, 512)],
                    start=True, stop=False))(oc), deps=pdeps)
            for j in range(16):
                for oc in range(4):
                    P.emit("tensor", (lambda j, oc, tt: lambda t: t.matmul(
                        psb[oc][:, :],
                        yT_full[:, TOK * j + 128 * tt: TOK * j + 128 * tt + 128],
                        wo_full[:, D * j + 512 * oc: D * j + 512 * (oc + 1)],
                        start=False, stop=(j == 15)))(j, oc, tt),
                        inc=cT if (j == 15 and oc == 3) else None)
            k_wo = cT.n
            for oc in range(4):
                P.emit("vector", (lambda oc: lambda v: v.tensor_mul(
                    attn_t[:, bass.ts(oc, 512)], psb[oc][:, :], gaF[:, bass.ts(oc, 512)]))(oc),
                    deps=[(cT.sem, k_wo, cT.name)])
                ev8[oc] = cV.n
            x2deps = [d_x] + ([dma["x2a0"].dep()] if tt >= 1 else []) + (
                [(cS.sem, x2cons[tt - 1], cS.name)] if tt >= 1 else [])
            for oc in range(4):
                P.emit("vector", (lambda oc, tt: lambda v: v.tensor_add(
                    x2_t[tt % 2][:, bass.ts(oc, 512)], x_st2[tt % 2][:, bass.ts(oc, 512)],
                    attn_t[:, bass.ts(oc, 512)]))(oc, tt), deps=x2deps)
            k_x2 = cV.n
            P.dma("sync", x2_s[bass.ts(tt, 128), :], x2_t[tt % 2][:, :],
                  dma["x2a0"], deps=[(cV.sem, k_x2, cV.name)])
            P.emit("vector", (lambda tt: lambda v: v.reduce_sum(st2[:, 0:1], x2_t[tt % 2][:, :], axis=AX))(tt))
            k_mean2 = P.emit("vector", lambda v: v.tensor_scalar(st2[:, 2:3], st2[:, 0:1], RD, None, ALU.mult))
            k_sq2 = P.emit("scalar", (lambda tt: lambda s: s.activation(
                sq2[:, :], x2_t[tt % 2][:, :], AF.Square, accum_out=st2[:, 1:2]))(tt),
                deps=[(cV.sem, k_mean2, cV.name)])
            P.emit("vector", lambda v: v.tensor_scalar(st2[:, 7:8], st2[:, 1:2], RD, None, ALU.mult),
                   deps=[(cS.sem, k_sq2, cS.name)])
            P.emit("vector", lambda v: v.tensor_mul(st2[:, 3:4], st2[:, 2:3], st2[:, 2:3]))
            k_var2 = P.emit("vector", lambda v: v.tensor_sub(st2[:, 3:4], st2[:, 7:8], st2[:, 3:4]))
            k_std2 = P.emit("scalar", lambda s: s.activation(st2[:, 4:5], st2[:, 3:4], AF.Sqrt,
                                                             bias=epscol[:, 0:1]),
                            deps=[(cV.sem, k_var2, cV.name)])
            P.emit("vector", lambda v: v.reciprocal(st2[:, 5:6], st2[:, 4:5]),
                   deps=[(cS.sem, k_std2, cS.name)])
            P.emit("vector", lambda v: v.tensor_mul(st2[:, 7:8], st2[:, 2:3], st2[:, 5:6]))
            k_inv2 = P.emit("vector", lambda v: v.tensor_scalar(st2[:, 6:7], st2[:, 7:8], -1.0, None, ALU.mult))
            k_xh2 = P.emit("scalar", (lambda tt: lambda s: s.activation(
                xh2_t[:, :], x2_t[tt % 2][:, :], AF.Identity,
                bias=st2[:, 6:7], scale=st2[:, 5:6]))(tt),
                deps=[(cV.sem, k_inv2, cV.name)])
            v2_read.append(cV.n)
            x2cons.append(k_xh2)
            hdeps2 = [(cS.sem, k_xh2, cS.name), (cS.sem, k_bcast, cS.name)]
            if tt >= 1:
                hdeps2 += h2_free[tt - 1]
            P.emit("vector", (lambda tt: lambda v: v.tensor_mul(h2_t[tt % 2][:, :], xh2_t[:, :], cb1F[:, :]))(tt),
                   deps=hdeps2)
            P.emit("vector", (lambda tt: lambda v: v.tensor_add(h2_t[tt % 2][:, :], h2_t[tt % 2][:, :], sbF[:, :]))(tt))
            P.emit("vector", (lambda tt: lambda v: v.reduce_max(st2[:, 8:9], h2_t[tt % 2][:, :], axis=AX,
                                                                apply_absolute_value=True))(tt))
            k_ss1 = P.emit("scalar", (lambda tt: lambda s: s.activation(
                sq2[:, :], h2_t[tt % 2][:, :], AF.Square, accum_out=st2[:, 9:10]))(tt),
                deps=[(cV.sem, cV.n, cV.name)])
            k_n1 = P.emit("scalar", lambda s: s.activation(st2[:, 10:11], st2[:, 9:10], AF.Sqrt),
                          deps=[(cS.sem, k_ss1, cS.name)])
            P.emit("vector", lambda v: v.tensor_scalar_max(st2[:, 10:11], st2[:, 10:11], 1e-12),
                   deps=[(cS.sem, k_n1, cS.name)])
            P.emit("vector", lambda v: v.reciprocal(st2[:, 11:12], st2[:, 10:11]))
            P.emit("vector", lambda v: v.tensor_mul(st2[:, 8:9], st2[:, 8:9], st2[:, 11:12]))
            P.emit("vector", lambda v: v.tensor_scalar(st2[:, 8:9], st2[:, 8:9], SQD, None, ALU.mult))
            P.emit("vector", lambda v: v.tensor_scalar_max(st2[:, 8:9], st2[:, 8:9], 1e-5))
            P.emit("vector", (lambda tt: lambda v: v.tensor_scalar(
                drs1_all[:, tt:tt + 1], st2[:, 8:9], 1.0 / 127.0, None, ALU.mult))(tt))
            P.emit("vector", lambda v: v.reciprocal(st2[:, 9:10], st2[:, 8:9]))
            P.emit("vector", lambda v: v.tensor_mul(st2[:, 9:10], st2[:, 9:10], st2[:, 11:12]))
            P.emit("vector", lambda v: v.tensor_scalar(st2[:, 9:10], st2[:, 9:10], 127.0 * SQD, None, ALU.mult))
            P.emit("vector", (lambda tt: lambda v: v.tensor_scalar(
                rtmp[:, :], h2_t[tt % 2][:, :], st2[:, 9:10], MAGIC, ALU.mult, ALU.add))(tt))
            xrdeps = [dma["xr10"].dep()] if tt >= 1 else []
            k_xr1 = P.emit("vector", (lambda tt: lambda v: v.tensor_scalar(
                xr1_t[tt % 2][:, :], rtmp[:, :], -MAGIC, None, ALU.add))(tt), deps=xrdeps)
            h2_free.append([(cV.sem, k_xr1, cV.name), (cS.sem, k_ss1, cS.name)])
            P.dma("sync", Xr1_s[bass.ts(tt, 128), :], xr1_t[tt % 2][:, :],
                  dma["xr10"], deps=[(cV.sem, k_xr1, cV.name)])
            for _ in range(17):
                next(qg1, None)
        for _ in qg1:
            pass
        k_s1 = P.emit("vector", lambda v: v.tensor_scalar(
            s1_all[:, :], drs1_all[:, :], wsc_c[:, 1:2], None, ALU.mult))
        d_xr1_all = [dma["xr10"].dep()]
        d_x2_all = [dma["x2a0"].dep()]
        d_w1r_all = [dma["wr0"].dep(), dma["wr1"].dep()]

        esB.close()
        esD = contextlib.ExitStack()
        TD = lambda name, shape, dt: esD.enter_context(nc.sbuf_tensor(name, shape, dt))
        xr1T = TD("xr1T", [128, 16 * TOK], BF16)
        w1q_st = [TD(f"w1q_st{k}", [128, 16 * 512], BF16) for k in range(2)]
        g1_t = TD("g1_t", [128, 512], F32)
        u2_t = [TD(f"u2_t{k}", [128, 512], BF16) for k in range(2)]
        wqD_f32 = [TD(f"wqD_f32{k}", [128, CQ], F32) for k in range(2)]
        wqD_out = [TD(f"wqD_out{k}", [128, CQ], BF16) for k in range(2)]

        ovD = [dma[nm].dep() for nm in ("yld", "wol", "xa0", "xa1", "x2a0",
                                        "xr10", "wfa0", "wfa1", "wr0", "wr1")] + [
            (cT.sem, cT.n, cT.name), (cS.sem, cS.n, cS.name), (cV.sem, cV.n, cV.name)]
        qg2 = quant_weight_gen((wqD_f32, wqD_out),
                               w2T_i, w2r_s, 64, 2, 2, ovD)
        for i in range(16):
            P.dma("sync", xr1T[:, bass.ts(i, TOK)], Xr1_s[:, bass.ts(i, 128)],
                  dma["x1T"], deps=ovD + d_xr1_all, transpose=True)
        d_x1T = dma["x1T"].dep()

        u2k = {}
        ev10s = {}
        for og16 in range(16):
            wdep = list(ovD) if og16 < 2 else [(cT.sem, u2k[og16 - 2], cT.name)]
            for i in range(16):
                P.dma("gpsimd", w1q_st[og16 % 2][:, bass.ts(i, 512)],
                      w1r_s[bass.ts(i, 128), bass.ts(og16, 512)],
                      dma[f"w1a{og16 % 2}"], deps=wdep + d_w1r_all)
            d_w1 = dma[f"w1a{og16 % 2}"].dep()
            for tt in range(NTO):
                mdeps = [d_x1T, d_w1]
                sl = tt % 2
                if sl in ev10s:
                    mdeps.append((cS.sem, ev10s[sl], cS.name))
                for i in range(16):
                    P.emit("tensor", (lambda og16, tt, i, sl: lambda t: t.matmul(
                        psb[sl][:, :],
                        xr1T[:, TOK * i + 128 * tt: TOK * i + 128 * tt + 128],
                        w1q_st[og16 % 2][:, bass.ts(i, 512)],
                        start=(i == 0), stop=(i == 15)))(og16, tt, i, sl),
                        deps=mdeps if i == 0 else (), inc=cT if i == 15 else None)
                k_m1 = cT.n
                P.emit("scalar", (lambda tt, sl: lambda s: s.activation(
                    g1_t[:, :], psb[sl][:, :], AF.Gelu, scale=s1_all[:, tt:tt + 1]))(tt, sl),
                    deps=[(cT.sem, k_m1, cT.name), (cV.sem, k_s1, cV.name)])
                ev10s[sl] = cS.n
                u2dep = [dma[f"u2s{tt % 2}"].dep()] if (og16 * NTO + tt) >= 2 else []
                k_u2 = P.emit("scalar", (lambda tt: lambda s: s.activation(
                    u2_t[tt % 2][:, :], g1_t[:, :], AF.Gelu))(tt), deps=u2dep)
                P.dma("sync", u2_s[bass.ts(tt, 128), bass.ts(og16, 512)],
                      u2_t[tt % 2][:, :], dma[f"u2s{tt % 2}"], deps=[(cS.sem, k_u2, cS.name)])
            u2k[og16] = cT.n
            for _ in range(17):
                next(qg2, None)
        for _ in qg2:
            pass
        d_u2_all = [dma["u2s0"].dep(), dma["u2s1"].dep()]
        d_wr_all = [dma["wr0"].dep(), dma["wr1"].dep()]

        esD.close()
        esE = contextlib.ExitStack()
        TE2 = lambda name, shape, dt: esE.enter_context(nc.sbuf_tensor(name, shape, dt))
        u2f = [TE2(f"u2f{k}", [128, INNER], BF16) for k in range(2)]
        sqE = TE2("sqE", [128, INNER], F32)
        rtE = TE2("rtE", [128, INNER], F32)
        xr2_t = [TE2(f"xr2_t{k}", [128, INNER], BF16) for k in range(2)]

        ovE = [dma[nm].dep() for nm in ("w1a0", "w1a1", "x1T", "u2s0", "u2s1",
                                        "wfa0", "wfa1", "wr0", "wr1")] + [
            (cT.sem, cT.n, cT.name), (cS.sem, cS.n, cS.name), (cV.sem, cV.n, cV.name)]
        v3_read = []
        for tt in range(NTO):
            ld = list(ovE) + d_u2_all
            if tt >= 2:
                ld.append((cV.sem, v3_read[tt - 2], cV.name))
            d_u = P.dma("gpsimd", u2f[tt % 2][:, :], u2_s[bass.ts(tt, 128), :],
                        dma[f"u2a{tt % 2}"], deps=ld)
            P.emit("vector", (lambda tt: lambda v: v.reduce_max(st2[:, 8:9], u2f[tt % 2][:, :], axis=AX,
                                                                apply_absolute_value=True))(tt),
                   deps=[d_u])
            k_ss2 = P.emit("scalar", (lambda tt: lambda s: s.activation(
                sqE[:, :], u2f[tt % 2][:, :], AF.Square, accum_out=st2[:, 9:10]))(tt),
                deps=[d_u, (cV.sem, cV.n, cV.name)])
            k_n2 = P.emit("scalar", lambda s: s.activation(st2[:, 10:11], st2[:, 9:10], AF.Sqrt),
                          deps=[(cS.sem, k_ss2, cS.name)])
            P.emit("vector", lambda v: v.tensor_scalar_max(st2[:, 10:11], st2[:, 10:11], 1e-12),
                   deps=[(cS.sem, k_n2, cS.name)])
            P.emit("vector", lambda v: v.reciprocal(st2[:, 11:12], st2[:, 10:11]))
            P.emit("vector", lambda v: v.tensor_mul(st2[:, 8:9], st2[:, 8:9], st2[:, 11:12]))
            P.emit("vector", lambda v: v.tensor_scalar(st2[:, 8:9], st2[:, 8:9], SQI, None, ALU.mult))
            P.emit("vector", lambda v: v.tensor_scalar_max(st2[:, 8:9], st2[:, 8:9], 1e-5))
            P.emit("vector", lambda v: v.tensor_scalar(st2[:, 7:8], st2[:, 8:9], 1.0 / 127.0, None, ALU.mult))
            P.emit("vector", lambda v: v.reciprocal(st2[:, 9:10], st2[:, 8:9]))
            P.emit("vector", lambda v: v.tensor_mul(st2[:, 9:10], st2[:, 9:10], st2[:, 11:12]))
            P.emit("vector", lambda v: v.tensor_scalar(st2[:, 9:10], st2[:, 9:10], 127.0 * SQI, None, ALU.mult))
            P.emit("vector", (lambda tt: lambda v: v.tensor_scalar(
                rtE[:, :], u2f[tt % 2][:, :], st2[:, 9:10], MAGIC, ALU.mult, ALU.add))(tt))
            xdep = [dma[f"x2r{tt % 2}"].dep()] if tt >= 2 else []
            k_x2r = P.emit("vector", (lambda tt: lambda v: v.tensor_scalar(
                xr2_t[tt % 2][:, :], rtE[:, :], -MAGIC, st2[:, 7:8], ALU.add, ALU.mult))(tt),
                deps=xdep)
            v3_read.append(cV.n)
            P.dma("sync", Xr2_s[bass.ts(tt, 128), :], xr2_t[tt % 2][:, :],
                  dma[f"x2r{tt % 2}"], deps=[(cV.sem, k_x2r, cV.name)])
        d_xr2_all = [dma["x2r0"].dep(), dma["x2r1"].dep()]
        if "d_x2" in dbg:
            for tt in range(NTO):
                dd = P.dma("gpsimd", u2f[0][:, 0:2 * D].bitcast(F32),
                           x2_s[bass.ts(tt, 128), :],
                           dma["u2a0"], deps=[(cV.sem, cV.n, cV.name), d_x2_all[0], dma["dbg"].dep()])
                P.dma("sync", dbg["d_x2"][bass.ts(tt, 128), :],
                      u2f[0][:, 0:2 * D].bitcast(F32), dma["dbg"], deps=[dd])
            P.dma("sync", dbg["d_xr1"], xr1_t[0][:, :], dma["dbg"], deps=[(cV.sem, cV.n, cV.name)])
            P.dma("sync", dbg["d_u2"], u2f[1][:, :], dma["dbg"])
            P.dma("sync", dbg["d_xr2"], xr2_t[1][:, :], dma["dbg"])
            P.dma("sync", dbg["d_drs"], drs1_all[:, :], dma["dbg"])
            P.dma("sync", dbg["d_wsc"], wsc_c[:, :], dma["dbg"])
            ddw = P.dma("gpsimd", u2f[0][:, :], w1r_s[1920:2048, :], dma["u2a0"],
                        deps=[dma["dbg"].dep()])
            P.dma("sync", dbg["d_w1r"], u2f[0][:, :], dma["dbg"], deps=[ddw])
            ddw2 = P.dma("gpsimd", u2f[0][:, 0:D], w2r_s[INNER - 128:INNER, :], dma["u2a0"],
                         deps=[dma["dbg"].dep()])
            P.dma("sync", dbg["d_w2r"], u2f[0][:, 0:D], dma["dbg"], deps=[ddw2])

        esE.close()
        esF = contextlib.ExitStack()
        TF = lambda name, shape, dt: esF.enter_context(nc.sbuf_tensor(name, shape, dt))
        xr2T = TF("xr2T", [128, 64 * 1024], BF16)
        w2q_st = [TF(f"w2q_st{k}", [128, 256], BF16) for k in range(2)]
        o2f_t = [TF(f"o2f_t{k}", [128, 512], BF16) for k in range(4)]
        x2l_t = [TF(f"x2l_t{k}", [128, 256], F32) for k in range(2)]
        gtk = TF("gtk", [128, 256], F32)
        o2tok = [TF(f"o2tok{k}", [128, 256], F32) for k in range(2)]
        x3_t = [TF(f"x3_t{k}", [128, 256], F32) for k in range(2)]

        ovF = [dma[nm].dep() for nm in ("u2a0", "u2a1", "x2r0", "x2r1", "dbg")] + [
            (cT.sem, cT.n, cT.name), (cS.sem, cS.n, cS.name), (cV.sem, cV.n, cV.name)]
        w2g = {}
        ev12 = {}
        o2f_free = {}
        tr_free = {}
        k_mm2_half = 0
        for tqh in range(2):
            xdep = list(ovF) + d_xr2_all
            if tqh:
                xdep.append((cT.sem, k_mm2_half, cT.name))
            for i in range(64):
                P.dma("sync", xr2T[:, bass.ts(i, 1024)],
                      Xr2_s[1024 * tqh: 1024 * (tqh + 1), bass.ts(i, 128)],
                      dma["x2T"], deps=xdep, transpose=True)
            d_x2T = dma["x2T"].dep()
            for o2p in range(8):
                for i in range(64):
                    wdep = [d_x2T] + d_wr_all
                    sl = i % 2
                    if sl in w2g:
                        wdep.append((cT.sem, w2g[sl], cT.name))
                    P.dma("gpsimd", w2q_st[sl][:, :],
                          w2r_s[bass.ts(i, 128), 256 * o2p: 256 * (o2p + 1)],
                          dma[f"w2a{sl}"], deps=wdep)
                    d_w2 = dma[f"w2a{sl}"].dep()
                    for o2c in range(2):
                        for tg in range(2):
                            pj = o2c * 2 + tg
                            mdeps = [d_w2]
                            if i == 0 and pj in ev12:
                                mdeps.append((cS.sem, ev12[pj], cS.name))
                            P.emit("tensor", (lambda i, o2c, tg, pj, sl: lambda t: t.matmul(
                                psb[pj][:, :],
                                w2q_st[sl][:, bass.ts(o2c, 128)],
                                xr2T[:, 1024 * i + 512 * tg: 1024 * i + 512 * (tg + 1)],
                                start=(i == 0), stop=(i == 63)))(i, o2c, tg, pj, sl),
                                deps=mdeps,
                                inc=cT if (o2c == 1 and tg == 1) else None)
                    w2g[sl] = cT.n
                k_mm2 = cT.n
                for o2c in range(2):
                    for tg in range(2):
                        pj = o2c * 2 + tg
                        edep = [(cT.sem, k_mm2, cT.name)]
                        if pj in o2f_free:
                            edep.append((cT.sem, o2f_free[pj], cT.name))
                        ev12[pj] = P.emit("scalar", (lambda pj: lambda s: s.activation(
                            o2f_t[pj][:, :], psb[pj][:, :],
                            AF.Identity, bias=0.0, scale=wsc_c[:, 3:4]))(pj),
                            deps=edep)
                for tg in range(2):
                    for tt4 in range(4):
                        tile = 8 * tqh + 4 * tg + tt4
                        par = tt4 % 2
                        for o2c in range(2):
                            pj = o2c * 2 + tg
                            slot = 2 * par + o2c
                            tdep = [(cS.sem, ev12[pj], cS.name), d_cst]
                            if slot in tr_free:
                                tdep.append((cV.sem, tr_free[slot], cV.name))
                            P.emit("tensor", (lambda pj, tt4, par, o2c: lambda t: t.transpose(
                                psb[4 + par][:, :].bitcast(BF16)[:, 128 * o2c: 128 * o2c + 128],
                                o2f_t[pj][:, 128 * tt4: 128 * tt4 + 128],
                                id_sb[:, :]))(pj, tt4, par, o2c), deps=tdep, inc=cT)
                        k_tr2 = cT.n
                        o2f_free[2 * 0 + tg] = k_tr2
                        o2f_free[2 * 1 + tg] = k_tr2
                        d_x2c = P.dma("gpsimd", x2l_t[par][:, :],
                                      x2_s[bass.ts(tile, 128), 256 * o2p: 256 * (o2p + 1)],
                                      dma[f"x2b{par}"],
                                      deps=d_x2_all + [(cV.sem, tr_free.get(2 * par, 0), cV.name)])
                        P.emit("vector", (lambda par: lambda v: v.tensor_copy(
                            o2tok[par][:, :], psb[4 + par][:, :].bitcast(BF16)[:, 0:256]))(par),
                            deps=[(cT.sem, k_tr2, cT.name)])
                        P.emit("vector", (lambda par, o2p: lambda v: v.tensor_mul(
                            gtk[:, :], o2tok[par][:, :],
                            gbF[:, 256 * o2p: 256 * (o2p + 1)]))(par, o2p))
                        x3dep = [d_x2c] + ([dma[f"ob{par}"].dep()] if True else [])
                        k_x3 = P.emit("vector", (lambda par: lambda v: v.tensor_add(
                            x3_t[par][:, :], x2l_t[par][:, :], gtk[:, :]))(par), deps=x3dep)
                        tr_free[2 * par] = k_x3
                        tr_free[2 * par + 1] = k_x3
                        P.dma("sync", out[bass.ts(tile, 128), 256 * o2p: 256 * (o2p + 1)],
                              x3_t[par][:, :], dma[f"ob{par}"], deps=[(cV.sem, k_x3, cV.name)])
            k_mm2_half = cT.n
        fin = [dma["ob0"].dep(), dma["ob1"].dep(), dma["dbg"].dep()]
        P.emit("sync", lambda e: [e.wait_ge(sm_, v_) for sm_, v_, _ in fin][-1])
        esF.close()
        block = E(nc.Block())

        @block.tensor
        def _(t):
            for fn in P.ops["tensor"]:
                fn(t)

        @block.vector
        def _(v):
            for fn in P.ops["vector"]:
                fn(v)

        @block.scalar
        def _(s):
            for fn in P.ops["scalar"]:
                fn(s)

        @block.gpsimd
        def _(g):
            for fn in P.ops["gpsimd"]:
                fn(g)

        @block.sync
        def _(sy):
            for fn in P.ops["sync"]:
                fn(sy)

    return nc, dbg


def kernel(**inputs):
    cores = list(range(8))
    in_maps = [make_core_inputs(i, inputs) for i in cores]
    nc, _ = build()
    res = run_bass_kernel_spmd(nc, in_maps, core_ids=cores)
    outs = [res.results[i]["out"] for i in range(8)]
    full = np.empty((B, L, D), np.float32)
    for i in cores:
        b, s0 = i // 4, (i % 4) * TOK
        full[b, s0:s0 + TOK] = outs[i]
    return full
